# version 17
# baseline (speedup 1.0000x reference)
import numpy as np
import ml_dtypes
from contextlib import ExitStack
import concourse.bass as bass
import concourse.mybir as mybir
from concourse.bass_utils import run_bass_kernel_spmd

F32 = mybir.dt.float32
BF16 = mybir.dt.bfloat16
AF = mybir.ActivationFunctionType
ALU = mybir.AluOpType
AX = mybir.AxisListType

NEG = -1.0e30


class Cfg:
    def __init__(self, T=2048, H=1024, I=2048, NH=16, E=8, SEQ=4):
        self.T, self.H, self.I, self.NH, self.E, self.SEQ = T, H, I, NH, E, SEQ
        self.HS = H // NH
        assert self.HS == 64 and H % 128 == 0 and I % 256 == 0
        self.NTOK = T * SEQ
        self.HC = H // 128
        self.ROPE_P = T // 2
        sets = []
        L = 4
        while L <= T:
            for i in range(T // L):
                sets.append((i * L, (i + 1) * L - 1))
            L *= 2
        self.sets = sets
        self.S = len(sets)
        self.SP1 = self.S + 1
        assert self.SP1 % 128 == 0
        self.GS = min(512, self.NTOK)
        self.FT = min(1024, self.NTOK)


REAL = Cfg()


def cdiv(a, b):
    return (a + b - 1) // b


def build_host_constants(cfg):
    T, SP1 = cfg.T, cfg.SP1
    r = np.array([b for (_, b) in cfg.sets], dtype=np.int64)
    t = np.arange(T)
    m = np.where(t[:, None] >= r[None, :], 0.0, NEG).astype(np.float32)
    mask_add = np.concatenate([m, np.zeros((T, 1), np.float32)], axis=1)
    mT = np.where(t[None, :] >= r[:, None], 0.0, NEG).astype(np.float32)
    maskT_add = np.concatenate([mT, np.full((1, T), NEG, np.float32)], axis=0)
    m4a = np.broadcast_to(((t % 4) >= 1).astype(np.float32), (64, T)).copy()
    m4b = np.broadcast_to(((t % 4) >= 2).astype(np.float32), (64, T)).copy()
    idn = np.eye(128, dtype=np.float32)
    ones_col = np.ones((128, 1), np.float32)
    ones_row = np.ones((1, 128), np.float32)
    esel = np.zeros((8, 8 * 128), np.float32)
    for e in range(8):
        esel[e, e * 128:(e + 1) * 128] = 1.0
    e64 = np.zeros((1, 128), np.float32)
    e64[0, 64] = 1.0
    return dict(c_mask_add=mask_add, c_maskT_add=maskT_add, c_m4a=m4a,
                c_m4b=m4b, c_idn=idn, c_ones_col=ones_col,
                c_ones_row=ones_row, c_esel=esel, c_e64=e64)


class SemC:
    __slots__ = ("h", "count")
    def __init__(self, h):
        self.h, self.count = h, 0


class Buf:
    __slots__ = ("name", "w", "readers")
    def __init__(self, name):
        self.name, self.w, self.readers = name, None, []


class Eng:
    def __init__(self, ctx, name):
        self.ctx, self.name = ctx, name
        self.items = []
        self.sem = None
        self.dsem = None
        self.waited = {}

    def wait(self, mark):
        self._deps_from([mark])

    def _deps_from(self, marks):
        best = {}
        for m in marks:
            if m is None:
                continue
            sc, val = m
            if val <= 0:
                continue
            prev = self.waited.get(id(sc))
            if prev is not None and prev[1] >= val:
                continue
            cur = best.get(id(sc))
            if cur is None or cur[1] < val:
                best[id(sc)] = (sc, val)
        for sc, val in best.values():
            self.waited[id(sc)] = (sc, val)
            self.items.append(lambda e, sc=sc, val=val: e.wait_ge(sc.h, val))

    def _deps(self, ins, outs):
        marks = [b.w for b in ins]
        for b in outs:
            marks.append(b.w)
            marks.extend(b.readers)
        self._deps_from(marks)

    def op(self, f, ins=(), outs=()):
        self._deps(ins, outs)
        self.sem.count += 1
        sc, cnt = self.sem, self.sem.count
        self.items.append(lambda e, f=f, sc=sc: f(e).then_inc(sc.h, 1))
        mark = (sc, cnt)
        for b in outs:
            b.w, b.readers = mark, []
        for b in ins:
            b.readers.append(mark)

    def dma(self, f, q, ins=(), outs=()):
        self._deps(ins, outs)
        q.count += 16
        cnt = q.count
        self.items.append(lambda e, f=f, q=q: f(e).then_inc(q.h, 16))
        mark = (q, cnt)
        for b in outs:
            b.w, b.readers = mark, []
        for b in ins:
            b.readers.append(mark)


class Ctx:
    def __init__(self, nc):
        self.nc = nc
        self.sem_guards = []
        self.all_sems = []
        self.bufs = {}
        self.pe = Eng(self, "pe")
        self.dve = Eng(self, "dve")
        self.act = Eng(self, "act")
        self.pool = Eng(self, "pool")
        self.sp = Eng(self, "sp")
        self.engines = [self.pe, self.dve, self.act, self.pool, self.sp]
        self.nsem = 0
        self._init_engine_sems = False

    def new_sem(self):
        cm = self.nc.semaphore(f"s{self.nsem}")
        self.nsem += 1
        h = cm.__enter__()
        self.sem_guards.append(cm)
        sc = SemC(h)
        self.all_sems.append(sc)
        return sc

    def new_phase_sems(self):
        if not self._init_engine_sems:
            for e in self.engines:
                e.sem = self.new_sem()
            self._init_engine_sems = True

    def queue(self, name):
        q = self.bufs.get("##q_" + name)
        if q is None:
            q = self.new_sem()
            self.bufs["##q_" + name] = q
        return q

    def buf(self, name):
        b = self.bufs.get(name)
        if b is None:
            b = Buf(name)
            self.bufs[name] = b
        return b

    def barrier(self):
        for e in self.engines:
            for sc in self.all_sems:
                if sc.count > 0:
                    e.wait((sc, sc.count))


def build_program(cfg, phases=("p1", "p2", "p3", "p4", "p5", "p6", "p7", "p8")):
    nc = bass.Bass("TRN2", target_bir_lowering=False)
    T, H, I, NH, E, SEQ = cfg.T, cfg.H, cfg.I, cfg.NH, cfg.E, cfg.SEQ
    HC, NTOK, SP1, GS, FT = cfg.HC, cfg.NTOK, cfg.SP1, cfg.GS, cfg.FT
    HS = cfg.HS
    SPL = I // 512 if I >= 1024 else 2
    IH = I // SPL
    IC2 = IH // 128
    scal = 0.125

    dt = nc.dram_tensor
    x_in = dt("x_fm", [H, NTOK], F32, kind="ExternalInput").ap()
    out_fm = dt("out_fm", [H, NTOK], F32, kind="ExternalOutput").ap()
    w_attnT = dt("w_attnT", [H, 3 * H], F32, kind="ExternalInput").ap()
    w_projT = dt("w_projT", [H, H], F32, kind="ExternalInput").ap()
    cos2 = dt("cos2", [128, cfg.ROPE_P], F32, kind="ExternalInput").ap()
    sin2 = dt("sin2", [128, cfg.ROPE_P], F32, kind="ExternalInput").ap()
    wq = dt("wq", [H, H], F32, kind="ExternalInput").ap()
    wk = dt("wk", [H, H], F32, kind="ExternalInput").ap()
    wv = dt("wv", [H, H], F32, kind="ExternalInput").ap()
    wo = dt("wo", [H, H], F32, kind="ExternalInput").ap()
    bq = dt("bq", [128, H // 128], F32, kind="ExternalInput").ap()
    bv = dt("bv", [H, 1], F32, kind="ExternalInput").ap()
    bo = dt("bo", [128, H // 128], F32, kind="ExternalInput").ap()
    gate_w = dt("gate_w", [H, E], F32, kind="ExternalInput").ap()
    sh_gate = dt("sh_gate", [H, I], F32, kind="ExternalInput").ap()
    sh_up = dt("sh_up", [H, I], F32, kind="ExternalInput").ap()
    sh_down = dt("sh_down", [I, H], F32, kind="ExternalInput").ap()
    rt_gate = dt("rt_gate", [E, H, I], F32, kind="ExternalInput").ap()
    rt_up = dt("rt_up", [E, H, I], F32, kind="ExternalInput").ap()
    rt_down = dt("rt_down", [E, I, H], F32, kind="ExternalInput").ap()

    hostc = build_host_constants(cfg)
    cst = {k: dt(k, list(v.shape), F32, kind="ExternalInput").ap()
           for k, v in hostc.items()}

    X = dt("Xres", [H, NTOK], F32).ap()
    QKVa = dt("QKVa", [3 * H, NTOK], BF16).ap()
    AO = dt("AOscr", [H, NTOK], BF16).ap()
    Qb = dt("Qbscr", [H, NTOK], BF16).ap()
    Kb = dt("Kbscr", [H, NTOK], BF16).ap()
    NG = NTOK // 128
    Vb = dt("Vbscr", [NH, NG, 128, HS], BF16).ap()
    NQC = T // 128
    STROW = dt("strow", [2, NQC * 256], F32).ap()
    STROW2 = dt("strow2", [4, 2 * T], BF16).ap()

    ctx = Ctx(nc)
    pe, dve, act, pool, sp = ctx.pe, ctx.dve, ctx.act, ctx.pool, ctx.sp
    esg = ExitStack()

    def sbuf(name, shape, dtyp, stk=esg):
        return stk.enter_context(nc.sbuf_tensor(name, shape, dtyp))

    def psum(name, shape, dtyp, stk=esg):
        return stk.enter_context(nc.psum_tensor(name, shape, dtyp))

    g_id = sbuf("g_id", [128, 128], BF16)
    g_idf = sbuf("g_idf", [128, 128], F32)
    g_oc = sbuf("g_oc", [128, 1], BF16)
    g_ocf = sbuf("g_ocf", [128, 1], F32)
    g_or = sbuf("g_or", [1, 128], BF16)
    g_orf = sbuf("g_orf", [1, 128], F32)
    g_esel = sbuf("g_esel", [8, 8 * 128], BF16)
    g_e64 = sbuf("g_e64", [1, 128], BF16)
    g_gw = sbuf("g_gw", [128, HC * E], BF16)
    g_bq = sbuf("g_bq", [128, HC], F32)
    g_bv = sbuf("g_bvr", [1, H], BF16)
    g_bo = sbuf("g_bo", [128, HC], F32)
    g_cos = sbuf("g_cos", [128, cfg.ROPE_P], BF16)
    g_sin = sbuf("g_sin", [128, cfg.ROPE_P], BF16)
    bC = ctx.buf("gconst")

    ctx.new_phase_sems()
    Qcp = ctx.queue("const_p")
    Qcs = ctx.queue("const_s")
    pool.dma(lambda e: e.dma_start(g_id[:, :], cst["c_idn"]), Qcp, outs=(bC,))
    pool.dma(lambda e: e.dma_start(g_oc[:, :], cst["c_ones_col"]), Qcp, outs=(bC,))
    pool.dma(lambda e: e.dma_start(g_or[:, :], cst["c_ones_row"]), Qcp, outs=(bC,))
    pool.dma(lambda e: e.dma_start(g_esel[:, :], cst["c_esel"]), Qcp, outs=(bC,))
    pool.dma(lambda e: e.dma_start(g_e64[:, :], cst["c_e64"]), Qcp, outs=(bC,))
    pool.dma(lambda e: e.dma_start(
        g_gw[:, :], gate_w.rearrange("(c p) n -> p c n", p=128)), Qcp, outs=(bC,))
    pool.dma(lambda e: e.dma_start(g_cos[:, :], cos2), Qcp, outs=(bC,))
    pool.dma(lambda e: e.dma_start(g_sin[:, :], sin2), Qcp, outs=(bC,))
    pool.dma(lambda e: e.dma_start(
        g_bv[:, :], bv.rearrange("a b -> b a")), Qcp, outs=(bC,))
    sp.dma(lambda e: e.dma_start(g_idf[:, :], cst["c_idn"]), Qcs, outs=(bC,))
    sp.dma(lambda e: e.dma_start(g_ocf[:, :], cst["c_ones_col"]), Qcs, outs=(bC,))
    sp.dma(lambda e: e.dma_start(g_orf[:, :], cst["c_ones_row"]), Qcs, outs=(bC,))
    sp.dma(lambda e: e.dma_start(g_bq[:, :], bq), Qcs, outs=(bC,))
    sp.dma(lambda e: e.dma_start(g_bo[:, :], bo), Qcs, outs=(bC,))
    bC.w = (Qcp, Qcp.count)
    ctx.sp.wait((Qcs, Qcs.count))
    bC2 = ctx.buf("gconst2")
    bC2.w = (Qcs, Qcs.count)

    class LNKit:
        def __init__(self, tag, TT, stk, pstk, ps_a=None, ps_b=None):
            self.TT = TT
            self.qx = ctx.queue(tag + "_qx")
            self.xt = sbuf(f"{tag}_xt", [128, HC * TT], F32, stk)
            self.xb = sbuf(f"{tag}_xb", [128, HC * TT], BF16, stk)
            self.sq = sbuf(f"{tag}_sq", [128, HC * TT], BF16, stk)
            self.rm = sbuf(f"{tag}_rm", [1, TT], F32, stk)
            self.rv = sbuf(f"{tag}_rv", [1, TT], F32, stk)
            self.rr = sbuf(f"{tag}_rr", [1, TT], F32, stk)
            self.rb = sbuf(f"{tag}_rb", [1, TT], F32, stk)
            self.pa = ps_a if ps_a is not None else psum(f"{tag}_pa", [128, 512], F32, pstk)
            self.pb = ps_b if ps_b is not None else psum(f"{tag}_pb", [128, 512], F32, pstk)
            self.pm = self.pa
            self.ps = self.pb
            self.b_xt = ctx.buf(f"{tag}_xt")
            self.b_xb = ctx.buf(f"{tag}_xb")
            self.b_sq = ctx.buf(f"{tag}_sq")
            self.b_pa = ctx.buf(f"{tag}_pa")
            self.b_pb = ctx.buf(f"{tag}_pb")
            self.b_pm = self.b_pa
            self.b_ps = self.b_pb
            self.b_rm = ctx.buf(f"{tag}_rm")
            self.b_rv = ctx.buf(f"{tag}_rv")
            self.b_rr = ctx.buf(f"{tag}_rr")
            self.b_rb = ctx.buf(f"{tag}_rb")

    def emit_ln(kit, src_dram, col0, ln_sb, b_ln, TT=None):
        TT = TT or kit.TT
        assert TT == kit.TT
        xt, xb, sq = kit.xt, kit.xb, kit.sq
        sp.dma(lambda e: e.dma_start(
            xt[:, :],
            src_dram[:, col0:col0 + TT].rearrange("(c p) n -> p c n", p=128)),
            kit.qx, outs=(kit.b_xt,))
        act.op(lambda e: e.copy(xb[:, 0:HC * TT], xt[:, 0:HC * TT]),
               ins=(kit.b_xt,), outs=(kit.b_xb,))
        act.op(lambda e: e.activation(sq[:, 0:HC * TT], xb[:, 0:HC * TT],
                                      AF.Square),
               ins=(kit.b_xb,), outs=(kit.b_sq,))
        rH = 1.0 / H
        ncg = cdiv(TT, 512)
        for cg in range(ncg):
            n0 = cg * 512
            n1 = min(TT, n0 + 512)
            w = n1 - n0
            def fm(e, n0=n0, w=w):
                for c in range(HC):
                    i = e.matmul(kit.pm[0:1, 0:w], g_oc[:, :],
                                 xb[:, c * TT + n0:c * TT + n0 + w],
                                 start=(c == 0), stop=(c == HC - 1))
                return i
            def fs(e, n0=n0, w=w):
                for c in range(HC):
                    i = e.matmul(kit.ps[0:1, 0:w], g_oc[:, :],
                                 sq[:, c * TT + n0:c * TT + n0 + w],
                                 start=(c == 0), stop=(c == HC - 1))
                return i
            pe.op(fm, ins=(kit.b_xb, bC), outs=(kit.b_pm,))
            pe.op(fs, ins=(kit.b_sq, bC), outs=(kit.b_ps,))
            dve.op(lambda e, n0=n0, w=w: e.tensor_scalar(
                kit.rm[0:1, n0:n0 + w], kit.pm[0:1, 0:w], rH, None, op0=ALU.mult),
                ins=(kit.b_pm,), outs=(kit.b_rm,))
            dve.op(lambda e, n0=n0, w=w: e.tensor_scalar(
                kit.rv[0:1, n0:n0 + w], kit.ps[0:1, 0:w], rH, None, op0=ALU.mult),
                ins=(kit.b_ps,), outs=(kit.b_rv,))
        dve.op(lambda e: e.tensor_mul(kit.rb[0:1, 0:TT], kit.rm[0:1, 0:TT],
                                      kit.rm[0:1, 0:TT]),
               ins=(kit.b_rm,), outs=(kit.b_rb,))
        dve.op(lambda e: e.tensor_sub(kit.rv[0:1, 0:TT], kit.rv[0:1, 0:TT],
                                      kit.rb[0:1, 0:TT]),
               ins=(kit.b_rv, kit.b_rb), outs=(kit.b_rv,))
        dve.op(lambda e: e.tensor_scalar(kit.rv[0:1, 0:TT], kit.rv[0:1, 0:TT],
                                         1e-5, None, op0=ALU.add),
               ins=(kit.b_rv,), outs=(kit.b_rv,))
        act.op(lambda e: e.activation(kit.rr[0:1, 0:TT], kit.rv[0:1, 0:TT],
                                      AF.Ln),
               ins=(kit.b_rv,), outs=(kit.b_rr,))
        act.op(lambda e: e.activation(kit.rr[0:1, 0:TT], kit.rr[0:1, 0:TT],
                                      AF.Exp, scale=-0.5),
               ins=(kit.b_rr,), outs=(kit.b_rr,))
        dve.op(lambda e: e.tensor_mul(kit.rb[0:1, 0:TT], kit.rm[0:1, 0:TT],
                                      kit.rr[0:1, 0:TT]),
               ins=(kit.b_rm, kit.b_rr), outs=(kit.b_rb,))
        for cg in range(ncg):
            n0 = cg * 512
            n1 = min(TT, n0 + 512)
            w = n1 - n0
            pe.op(lambda e, n0=n0, w=w: e.matmul(
                kit.pa[:, 0:w], g_orf[:, :], kit.rr[0:1, n0:n0 + w],
                start=True, stop=True),
                ins=(kit.b_rr, bC), outs=(kit.b_pa,))
            pe.op(lambda e, n0=n0, w=w: e.matmul(
                kit.pb[:, 0:w], g_orf[:, :], kit.rb[0:1, n0:n0 + w],
                start=True, stop=True),
                ins=(kit.b_rb, bC), outs=(kit.b_pb,))
            for c in range(HC):
                dve.op(lambda e, c=c, n0=n0, w=w: e.tensor_mul(
                    sq[:, c * TT + n0:c * TT + n0 + w],
                    xt[:, c * TT + n0:c * TT + n0 + w], kit.pa[:, 0:w]),
                    ins=(kit.b_xt, kit.b_pa), outs=(kit.b_sq,))
                dve.op(lambda e, c=c, n0=n0, w=w: e.tensor_sub(
                    ln_sb[:, c * TT + n0:c * TT + n0 + w],
                    sq[:, c * TT + n0:c * TT + n0 + w], kit.pb[:, 0:w]),
                    ins=(kit.b_sq, kit.b_pb), outs=(b_ln,))

    def phase_p1():
        ctx.new_phase_sems()
        stk, pstk = ExitStack(), ExitStack()
        JC = 3 * HC
        wsb = sbuf("p1_w", [128, HC * 3 * H], BF16, stk)
        bW = ctx.buf("p1_w")
        pool.dma(lambda e: e.dma_start(
            wsb[:, :], w_attnT.rearrange("(c p) n -> p c n", p=128)),
            ctx.queue("p1_w"), outs=(bW,))
        kit = LNKit("p1ln", GS, stk, pstk)
        Qst = [ctx.queue("p1_st0"), ctx.queue("p1_st1")]
        ln = [sbuf(f"p1_ln{i}", [128, HC * GS], BF16, stk) for i in range(2)]
        bL = [ctx.buf(f"p1_ln{i}") for i in range(2)]
        pq = [psum(f"p1_pq{i}", [128, GS], F32, pstk) for i in range(2)]
        bQ = [ctx.buf(f"p1_pq{i}") for i in range(2)]
        ev = [sbuf(f"p1_ev{i}", [128, GS], BF16, stk) for i in range(2)]
        bE = [ctx.buf(f"p1_ev{i}") for i in range(2)]
        for ti in range(NTOK // GS):
            par = ti % 2
            emit_ln(kit, x_in, ti * GS, ln[par], bL[par])
            for jc in range(JC):
                pp = jc % 2
                def fm(e, jc=jc, par=par, pp=pp):
                    for c in range(HC):
                        i = e.matmul(pq[pp][:, :],
                                     wsb[:, c * 3 * H + jc * 128:
                                         c * 3 * H + (jc + 1) * 128],
                                     ln[par][:, c * GS:(c + 1) * GS],
                                     start=(c == 0), stop=(c == HC - 1))
                    return i
                pe.op(fm, ins=(bW, bL[par]), outs=(bQ[pp],))
                if HC <= jc < 2 * HC:
                    act.op(lambda e, pp=pp: e.activation(
                        ev[pp][:, :], pq[pp][:, :], AF.Copy, scale=scal),
                        ins=(bQ[pp],), outs=(bE[pp],))
                else:
                    act.op(lambda e, pp=pp: e.copy(ev[pp][:, :], pq[pp][:, :]),
                           ins=(bQ[pp],), outs=(bE[pp],))
                sp.dma(lambda e, jc=jc, ti=ti, pp=pp: e.dma_start(
                    QKVa[jc * 128:(jc + 1) * 128, ti * GS:(ti + 1) * GS],
                    ev[pp][:, :]), Qst[pp], ins=(bE[pp],))
        ctx.barrier()
        stk.close()
        pstk.close()

    def phase_p2():
        ctx.new_phase_sems()
        stk, pstk = ExitStack(), ExitStack()
        NSC = SP1 // 128
        KCW = min(512, SP1)
        NKC = SP1 // KCW
        NQB = cdiv(T, 512)
        QB = min(512, T)
        maskP1 = sbuf("p2_m1", [128, NQC * SP1], BF16, stk)
        maskP2 = sbuf("p2_m2", [128, NSC * T], BF16, stk)
        m4a = sbuf("p2_m4a", [64, T], BF16, stk)
        m4b = sbuf("p2_m4b", [64, T], BF16, stk)
        bM = ctx.buf("p2_cst")
        Qm = ctx.queue("p2_mask")
        pool.dma(lambda e: e.dma_start(
            maskP1[:, :], cst["c_mask_add"].rearrange("(c p) n -> p c n", p=128)),
            Qm, outs=(bM,))
        pool.dma(lambda e: e.dma_start(
            maskP2[:, :], cst["c_maskT_add"].rearrange("(c p) n -> p c n", p=128)),
            Qm, outs=(bM,))
        pool.dma(lambda e: e.dma_start(m4a[:, :], cst["c_m4a"]), Qm, outs=(bM,))
        pool.dma(lambda e: e.dma_start(m4b[:, :], cst["c_m4b"]), Qm, outs=(bM,))
        bM.w = (Qm, Qm.count)
        Qlq = [ctx.queue("p2_lq0"), ctx.queue("p2_lq1")]
        Qlk = [ctx.queue("p2_lk0"), ctx.queue("p2_lk1")]
        Qlv = [ctx.queue("p2_lv0"), ctx.queue("p2_lv1")]
        QvT = [ctx.queue("p2_vT0"), ctx.queue("p2_vT1")]
        Qsr = [ctx.queue("p2_sr0"), ctx.queue("p2_sr1")]
        Qsrb = [ctx.queue("p2_srb0"), ctx.queue("p2_srb1")]
        Qao = [ctx.queue("p2_ao0"), ctx.queue("p2_ao1")]

        def mk2(nm, shape, dtyp):
            return ([sbuf(f"{nm}{i}", shape, dtyp, stk) for i in range(2)],
                    [ctx.buf(f"{nm}{i}") for i in range(2)])
        q2, bq2 = mk2("p2_q2", [66, T], BF16)
        kt, bkt = mk2("p2_kt", [64, T], BF16)
        vt, bvt = mk2("p2_vt", [64, T], BF16)
        ks, bks = mk2("p2_ks", [66, SP1], BF16)
        vs, bvs = mk2("p2_vs", [64, SP1], BF16)
        vT, bvT = mk2("p2_vT", [128, NSC * 64], BF16)
        ktl, bktl = mk2("p2_ktl", [64, T], BF16)
        vtlb, bvtlb = mk2("p2_vtlb", [64, T], BF16)
        pbt, bpbt = mk2("p2_pbt", [64, T], BF16)
        tre1 = sbuf("p2_tre", [64, T], BF16, stk)
        btre1 = ctx.buf("p2_tre")
        tre, btre = [tre1, tre1], [btre1, btre1]
        trs1 = sbuf("p2_trs", [64, SP1], F32, stk)
        btrs1 = ctx.buf("p2_trs")
        trs, btrs = [trs1, trs1], [btrs1, btrs1]
        tailr, btailr = mk2("p2_tailr", [1, T], BF16)
        strow_sb, bstrow = mk2("p2_str", [NQC, 256], F32)
        statr, bstatr = mk2("p2_statr", [1, T], F32)
        stlo, bstlo = mk2("p2_stlo", [1, T], F32)
        sfh, bsfh = mk2("p2_sfh", [128, NQC], BF16)
        sfl, bsfl = mk2("p2_sfl", [128, NQC], BF16)
        attP, battP = mk2("p2_attP", [1, T], BF16)
        ncv, bncv = mk2("p2_ncv", [128, NQC * NKC], F32)
        scv, bscv = mk2("p2_scv", [128, NQC * NKC], F32)
        stf, bstf = mk2("p2_stf", [128, NQC], F32)
        junk = sbuf("p2_junk", [128, KCW], BF16, stk)
        bjunk = ctx.buf("p2_junk")
        attb = [sbuf(f"p2_attb{i}", [128, QB], BF16, stk) for i in range(4)]
        battb = [ctx.buf(f"p2_attb{i}") for i in range(4)]
        osb, bosb = mk2("p2_osb", [64, QB], BF16)
        tcon, btcon = mk2("p2_tcon", [64, QB], BF16)
        sgl, bsgl = mk2("p2_sgl", [128, 3], F32)

        ps1 = [psum(f"p2_ps1{i}", [128, KCW], F32, pstk) for i in range(2)]
        bps1 = [ctx.buf(f"p2_ps1{i}") for i in range(2)]
        ps2 = [psum(f"p2_ps2{i}", [128, 512], F32, pstk) for i in range(4)]
        bps2 = [ctx.buf(f"p2_ps2{i}") for i in range(4)]
        pso2 = [psum(f"p2_pso{i}", [64, QB], F32, pstk) for i in range(2)]
        bpso2 = [ctx.buf(f"p2_pso{i}") for i in range(2)]

        rset = [b for (_, b) in cfg.sets]

        def gen(it_idx, s, h):
            p = it_idx % 2
            slot = it_idx % 2
            c0 = s * T
            psA, bpsA = ps2[slot * 2], bps2[slot * 2]
            pso, bpso = pso2[slot], bpso2[slot]
            sp.dma(lambda e, h=h, c0=c0, p=p: e.dma_start(
                q2[p][0:64, :], QKVa[h * HS:(h + 1) * HS, c0:c0 + T]),
                Qlq[p], outs=(bq2[p],))
            sp.dma(lambda e, h=h, c0=c0, p=p: e.dma_start(
                kt[p][:, :], QKVa[H + h * HS:H + (h + 1) * HS, c0:c0 + T]),
                Qlk[p], outs=(bkt[p],))
            sp.dma(lambda e, h=h, c0=c0, p=p: e.dma_start(
                vt[p][:, :], QKVa[2 * H + h * HS:2 * H + (h + 1) * HS,
                                  c0:c0 + T]), Qlv[p], outs=(bvt[p],))
            yield
            def emit_tree(srcb, bsrc, dstb, bdst, p=p):
                pool.op(lambda e: e.tensor_add(
                    tre[p][:, 0:T // 2], srcb[:, 0:T:2], srcb[:, 1:T:2]),
                    ins=(bsrc,), outs=(btre[p],))
                n = T // 4
                pool.op(lambda e, n=n: e.tensor_add(
                    trs[p][:, 0:n], tre[p][:, 0:T // 2:2],
                    tre[p][:, 1:T // 2:2]),
                    ins=(btre[p],), outs=(btrs[p],))
                po, pn = 0, n
                L = 8
                while L <= T:
                    n = T // L
                    off = po + pn
                    pool.op(lambda e, n=n, off=off, po=po, pn=pn:
                            e.tensor_add(trs[p][:, off:off + n],
                                         trs[p][:, po:po + pn:2],
                                         trs[p][:, po + 1:po + pn:2]),
                            ins=(btrs[p],), outs=(btrs[p],))
                    po, pn = off, n
                    L *= 2
                act.op(lambda e: e.copy(dstb[0:64, 0:SP1 - 1],
                                        trs[p][:, 0:SP1 - 1]),
                       ins=(btrs[p],), outs=(bdst,))
            emit_tree(kt[p], bkt[p], ks[p], bks[p])
            yield
            emit_tree(vt[p], bvt[p], vs[p], bvs[p])
            def fz(e, p=p):
                e.memset(ks[p][0:64, SP1 - 1:SP1], 0.0)
                e.memset(vs[p][:, SP1 - 1:SP1], 0.0)
                return e.memset(ks[p][64:66, :], 1.0)
            pool.op(fz, outs=(bks[p], bvs[p]))
            yield
            def emit_tails(srcb, bsrc, dstb, bdst, p=p):
                def t1(e):
                    e.tensor_mul(tre[p][:, 1:T], srcb[:, 0:T - 1],
                                 m4a[:, 1:T])
                    return e.memset(tre[p][:, 0:1], 0.0)
                pool.op(t1, ins=(bsrc, bM), outs=(btre[p],))
                pool.op(lambda e: e.tensor_add(tre[p][:, :], tre[p][:, :],
                                               srcb[:, :]),
                        ins=(btre[p], bsrc), outs=(btre[p],))
                def t2(e):
                    e.tensor_mul(dstb[:, 2:T], tre[p][:, 0:T - 2],
                                 m4b[:, 2:T])
                    return e.memset(dstb[:, 0:2], 0.0)
                pool.op(t2, ins=(btre[p], bM), outs=(bdst,))
                pool.op(lambda e: e.tensor_add(dstb[:, :], dstb[:, :],
                                               tre[p][:, :]),
                        ins=(bdst, btre[p]), outs=(bdst,))
            emit_tails(kt[p], bkt[p], ktl[p], bktl[p])
            yield
            emit_tails(vt[p], bvt[p], vtlb[p], bvtlb[p])
            yield
            for sc_i in range(NSC):
                def ftr(e, sc_i=sc_i, p=p):
                    return e.dma_start(
                        vT[p][:, sc_i * 64:(sc_i + 1) * 64],
                        vs[p][:, sc_i * 128:(sc_i + 1) * 128],
                        transpose=True)
                sp.dma(ftr, QvT[p], ins=(bvs[p],), outs=(bvT[p],))
            bvT[p].w = (QvT[p], QvT[p].count)
            pool.op(lambda e, p=p: e.tensor_mul(pbt[p][:, :], q2[p][0:64, :],
                                                ktl[p][:, :]),
                    ins=(bq2[p], bktl[p]), outs=(bpbt[p],))
            yield
            for cg in range(cdiv(T, 512)):
                n0 = cg * 512
                w = min(T, n0 + 512) - n0
                pe.op(lambda e, n0=n0, w=w, p=p, psA=psA: e.matmul(
                    psA[0:1, 0:w], g_oc[0:64, :], pbt[p][:, n0:n0 + w],
                    start=True, stop=True),
                    ins=(bpbt[p], bC), outs=(bpsA,))
                dve.op(lambda e, n0=n0, w=w, p=p, psA=psA: e.tensor_copy(
                    tailr[p][0:1, n0:n0 + w], psA[0:1, 0:w]),
                    ins=(bpsA,), outs=(btailr[p],))
            yield
            for qc in range(NQC):
                for kc in range(NKC):
                    pp = kc % 2
                    n0 = kc * KCW
                    def f1(e, qc=qc, kc=kc, n0=n0, p=p, pp=pp):
                        wset = KCW - 1 if kc == NKC - 1 else KCW
                        i = e.matmul(ps1[pp][:, 0:wset],
                                     q2[p][0:64, qc * 128:(qc + 1) * 128],
                                     ks[p][0:64, n0:n0 + wset],
                                     start=True, stop=True)
                        if kc == NKC - 1:
                            i = e.matmul(
                                ps1[pp][:, KCW - 1:KCW],
                                tailr[p][0:1, qc * 128:(qc + 1) * 128],
                                g_or[0:1, 0:1], start=True, stop=True)
                        return i
                    pe.op(f1, ins=(bq2[p], bks[p], btailr[p], bC),
                          outs=(bps1[pp],))
                    jj = qc * NKC + kc
                    dve.op(lambda e, qc=qc, kc=kc, pp=pp: e.tensor_add(
                        ps1[pp][:, :], ps1[pp][:, :],
                        maskP1[:, qc * SP1 + kc * KCW:
                               qc * SP1 + (kc + 1) * KCW]),
                        ins=(bM, bps1[pp]), outs=(bps1[pp],))
                    dve.op(lambda e, jj=jj, pp=pp, p=p: e.reduce_max(
                        ncv[p][:, jj:jj + 1], ps1[pp][:, :], axis=AX.X,
                        negate=True),
                        ins=(bps1[pp],), outs=(bncv[p],))
                    def f2(e, jj=jj, pp=pp, p=p):
                        return e.activation(
                            junk[:, :], ps1[pp][:, :], AF.Exp,
                            bias=ncv[p][:, jj:jj + 1],
                            accum_out=scv[p][:, jj:jj + 1])
                    act.op(f2, ins=(bps1[pp], bncv[p]),
                           outs=(bjunk, bscv[p]))
                jj0 = qc * NKC
                if NKC == 1:
                    act.op(lambda e, jj0=jj0, p=p: e.activation(
                        sgl[p][:, 2:3], scv[p][:, jj0:jj0 + 1], AF.Ln),
                        ins=(bscv[p],), outs=(bsgl[p],))
                    dve.op(lambda e, qc=qc, jj0=jj0, p=p: e.tensor_sub(
                        stf[p][:, qc:qc + 1], ncv[p][:, jj0:jj0 + 1],
                        sgl[p][:, 2:3]),
                        ins=(bncv[p], bsgl[p]), outs=(bstf[p],))
                else:
                    dve.op(lambda e, jj0=jj0, p=p: e.tensor_reduce(
                        sgl[p][:, 0:1], ncv[p][:, jj0:jj0 + NKC], axis=AX.X,
                        op=ALU.min),
                        ins=(bncv[p],), outs=(bsgl[p],))
                    dve.op(lambda e, jj0=jj0, p=p: e.tensor_scalar(
                        ncv[p][:, jj0:jj0 + NKC], ncv[p][:, jj0:jj0 + NKC],
                        sgl[p][:, 0:1], None, op0=ALU.subtract),
                        ins=(bncv[p], bsgl[p]), outs=(bncv[p],))
                    act.op(lambda e, jj0=jj0, p=p: e.activation(
                        ncv[p][:, jj0:jj0 + NKC], ncv[p][:, jj0:jj0 + NKC],
                        AF.Exp, scale=-1.0),
                        ins=(bncv[p],), outs=(bncv[p],))
                    dve.op(lambda e, jj0=jj0, p=p: e.tensor_mul(
                        scv[p][:, jj0:jj0 + NKC], scv[p][:, jj0:jj0 + NKC],
                        ncv[p][:, jj0:jj0 + NKC]),
                        ins=(bscv[p], bncv[p]), outs=(bscv[p],))
                    dve.op(lambda e, jj0=jj0, p=p: e.reduce_sum(
                        sgl[p][:, 1:2], scv[p][:, jj0:jj0 + NKC], axis=AX.X),
                        ins=(bscv[p],), outs=(bsgl[p],))
                    act.op(lambda e, p=p: e.activation(sgl[p][:, 2:3],
                                                       sgl[p][:, 1:2], AF.Ln),
                           ins=(bsgl[p],), outs=(bsgl[p],))
                    dve.op(lambda e, qc=qc, p=p: e.tensor_sub(
                        stf[p][:, qc:qc + 1], sgl[p][:, 0:1], sgl[p][:, 2:3]),
                        ins=(bsgl[p],), outs=(bstf[p],))
                yield
            act.op(lambda e, p=p: e.copy(sfh[p][:, :], stf[p][:, :]),
                   ins=(bstf[p],), outs=(bsfh[p],))
            dve.op(lambda e, p=p: e.tensor_sub(
                sfl[p][:, :], stf[p][:, :], sfh[p][:, :]),
                ins=(bstf[p], bsfh[p]), outs=(bsfl[p],))
            pe.op(lambda e, p=p, psA=psA: e.matmul(
                psA[0:NQC, 0:128], sfh[p][:, :], g_id[:, :],
                start=True, stop=True),
                ins=(bsfh[p], bC), outs=(bpsA,))
            dve.op(lambda e, p=p, psA=psA: e.tensor_copy(
                strow_sb[p][:, 0:128], psA[0:NQC, 0:128]),
                ins=(bpsA,), outs=(bstrow[p],))
            pe.op(lambda e, p=p, psA=psA: e.matmul(
                psA[0:NQC, 0:128], sfl[p][:, :], g_id[:, :],
                start=True, stop=True),
                ins=(bsfl[p], bC), outs=(bpsA,))
            dve.op(lambda e, p=p, psA=psA: e.tensor_copy(
                strow_sb[p][:, 128:256], psA[0:NQC, 0:128]),
                ins=(bpsA,), outs=(bstrow[p],))
            sp.dma(lambda e, p=p: e.dma_start(
                STROW[p:p + 1, :].rearrange("o (a b) -> (o a) b", b=256),
                strow_sb[p][:, :]), Qsr[p],
                ins=(bstrow[p],), outs=(bstrow[p],))
            sp.dma(lambda e, p=p: e.dma_start(
                statr[p][:, :],
                STROW[p:p + 1, :].rearrange("o (a b) -> (o a) b", b=256)[:, 0:128]),
                Qsr[p], ins=(bstrow[p],), outs=(bstatr[p],))
            sp.dma(lambda e, p=p: e.dma_start(
                stlo[p][:, :],
                STROW[p:p + 1, :].rearrange("o (a b) -> (o a) b", b=256)[:, 128:256]),
                Qsr[p], ins=(bstrow[p],), outs=(bstlo[p],))
            pool.dma(lambda e, p=p: e.dma_start(
                q2[p][64:65, :],
                STROW[p:p + 1, :].rearrange("o (a b) -> (o a) b", b=256)[:, 0:128]),
                Qsrb[p], ins=(bstrow[p],), outs=(bq2[p],))
            pool.dma(lambda e, p=p: e.dma_start(
                q2[p][65:66, :],
                STROW[p:p + 1, :].rearrange("o (a b) -> (o a) b", b=256)[:, 128:256]),
                Qsrb[p], ins=(bstrow[p],), outs=(bq2[p],))
            dve.op(lambda e, p=p: e.tensor_add(
                statr[p][0:1, :], statr[p][0:1, :], stlo[p][0:1, :]),
                ins=(bstatr[p], bstlo[p]), outs=(bstatr[p],))
            dve.op(lambda e, p=p: e.tensor_add(
                statr[p][0:1, :], statr[p][0:1, :], tailr[p][0:1, :]),
                ins=(bstatr[p], btailr[p]), outs=(bstatr[p],))
            act.op(lambda e, p=p: e.activation(
                attP[p][0:1, :], statr[p][0:1, :], AF.Exp),
                ins=(bstatr[p],), outs=(battP[p],))
            yield
            for qb in range(NQB):
                qn0 = qb * QB
                qmax = qn0 + QB - 1
                incl = []
                needm = []
                for sc_i in range(NSC):
                    lo = sc_i * 128
                    hi = min(SP1 - 1, (sc_i + 1) * 128)
                    rs = rset[lo:hi]
                    if not rs or min(rs) > qmax:
                        continue
                    incl.append(sc_i)
                    needm.append(max(rs) > qn0 or sc_i == NSC - 1)
                for ii, sc_i in enumerate(incl):
                    pb2 = slot * 2 + 1 if ii % 2 else slot * 2
                    pb2 = slot * 2 + (ii % 2)
                    def f5(e, sc_i=sc_i, qn0=qn0, p=p, pb2=pb2):
                        return e.matmul(
                            ps2[pb2][:, 0:QB],
                            ks[p][:, sc_i * 128:(sc_i + 1) * 128],
                            q2[p][:, qn0:qn0 + QB],
                            start=True, stop=True)
                    pe.op(f5, ins=(bks[p], bq2[p]), outs=(bps2[pb2],))
                    if needm[ii]:
                        dve.op(lambda e, sc_i=sc_i, qn0=qn0, pb2=pb2:
                               e.tensor_add(
                                   ps2[pb2][:, 0:QB], ps2[pb2][:, 0:QB],
                                   maskP2[:, sc_i * T + qn0:
                                          sc_i * T + qn0 + QB]),
                               ins=(bM, bps2[pb2]), outs=(bps2[pb2],))
                    ab = slot * 2 + (ii % 2)
                    act.op(lambda e, pb2=pb2, ab=ab: e.activation(
                        attb[ab][:, :], ps2[pb2][:, 0:QB], AF.Exp),
                        ins=(bps2[pb2],), outs=(battb[ab],))
                    yield
                    def f6(e, sc_i=sc_i, p=p, ab=ab, pso=pso,
                           first=(ii == 0), last=(ii == len(incl) - 1)):
                        return e.matmul(
                            pso[:, :], vT[p][:, sc_i * 64:(sc_i + 1) * 64],
                            attb[ab][:, :], start=first, stop=last)
                    pe.op(f6, ins=(bvT[p], battb[ab]), outs=(bpso,))
                pe.op(lambda e, qn0=qn0, p=p, psA=psA: e.matmul(
                    psA[0:64, 0:QB], g_or[0:1, 0:64],
                    attP[p][0:1, qn0:qn0 + QB], start=True, stop=True),
                    ins=(battP[p], bC), outs=(bpsA,))
                dve.op(lambda e, qn0=qn0, p=p, psA=psA: e.tensor_mul(
                    tcon[p][:, :], vtlb[p][:, qn0:qn0 + QB],
                    psA[0:64, 0:QB]),
                    ins=(bvtlb[p], bpsA), outs=(btcon[p],))
                dve.op(lambda e, p=p, pso=pso: e.tensor_add(
                    osb[p][:, :], pso[:, :], tcon[p][:, :]),
                    ins=(bpso, btcon[p]), outs=(bosb[p],))
                sp.dma(lambda e, h=h, c0=c0, qn0=qn0, p=p: e.dma_start(
                    AO[h * HS:(h + 1) * HS, c0 + qn0:c0 + qn0 + QB],
                    osb[p][:, :]), Qao[p], ins=(bosb[p],))
                yield

        gens = []
        it_idx = 0
        for s in range(SEQ):
            for h in range(NH):
                gens.append(gen(it_idx, s, h))
                it_idx += 1
        drive(gens, W=2)
        ctx.barrier()
        stk.close()
        pstk.close()

    def phase_proj(tag, wT_dram, res_src, dst, bias_sb=None):
        ctx.new_phase_sems()
        stk, pstk = ExitStack(), ExitStack()
        wsb = sbuf(f"{tag}_w", [128, HC * H], BF16, stk)
        bW = ctx.buf(f"{tag}_w")
        pool.dma(lambda e: e.dma_start(
            wsb[:, :], wT_dram.rearrange("(c p) n -> p c n", p=128)),
            ctx.queue(tag + "_w"), outs=(bW,))
        Qa = [ctx.queue(tag + "_a0"), ctx.queue(tag + "_a1")]
        Qx = [ctx.queue(tag + "_x0"), ctx.queue(tag + "_x1")]
        Qo = [ctx.queue(tag + "_o0"), ctx.queue(tag + "_o1")]
        ao = [sbuf(f"{tag}_ao{i}", [128, HC * GS], BF16, stk) for i in range(2)]
        bA = [ctx.buf(f"{tag}_ao{i}") for i in range(2)]
        xr = [sbuf(f"{tag}_xr{i}", [128, HC * GS], F32, stk) for i in range(2)]
        bX = [ctx.buf(f"{tag}_xr{i}") for i in range(2)]
        xo = [sbuf(f"{tag}_xo{i}", [128, HC * GS], F32, stk) for i in range(2)]
        bO = [ctx.buf(f"{tag}_xo{i}") for i in range(2)]
        pp_ = [psum(f"{tag}_pp{i}", [128, GS], F32, pstk) for i in range(2)]
        bP = [ctx.buf(f"{tag}_pp{i}") for i in range(2)]
        for ti in range(NTOK // GS):
            par = ti % 2
            sp.dma(lambda e, ti=ti, par=par: e.dma_start(
                ao[par][:, :],
                AO[:, ti * GS:(ti + 1) * GS].rearrange("(c p) n -> p c n", p=128)),
                Qa[par], outs=(bA[par],))
            sp.dma(lambda e, ti=ti, par=par: e.dma_start(
                xr[par][:, :],
                res_src[:, ti * GS:(ti + 1) * GS].rearrange("(c p) n -> p c n",
                                                            p=128)),
                Qx[par], outs=(bX[par],))
            for jc in range(HC):
                pp2 = jc % 2
                def fm(e, jc=jc, par=par, pp2=pp2):
                    for c in range(HC):
                        i = e.matmul(pp_[pp2][:, :],
                                     wsb[:, c * H + jc * 128:c * H + (jc + 1) * 128],
                                     ao[par][:, c * GS:(c + 1) * GS],
                                     start=(c == 0), stop=(c == HC - 1))
                    return i
                pe.op(fm, ins=(bW, bA[par]), outs=(bP[pp2],))
                if bias_sb is not None:
                    def fe(e, jc=jc, par=par, pp2=pp2):
                        return e.scalar_tensor_tensor(
                            xo[par][:, jc * GS:(jc + 1) * GS], pp_[pp2][:, :],
                            bias_sb[:, jc:jc + 1],
                            xr[par][:, jc * GS:(jc + 1) * GS],
                            op0=ALU.add, op1=ALU.add)
                else:
                    def fe(e, jc=jc, par=par, pp2=pp2):
                        return e.tensor_add(
                            xo[par][:, jc * GS:(jc + 1) * GS], pp_[pp2][:, :],
                            xr[par][:, jc * GS:(jc + 1) * GS])
                dve.op(fe, ins=(bP[pp2], bX[par], bC), outs=(bO[par],))
            sp.dma(lambda e, ti=ti, par=par: e.dma_start(
                dst[:, ti * GS:(ti + 1) * GS].rearrange("(c p) n -> p c n",
                                                        p=128),
                xo[par][:, :]), Qo[par], ins=(bO[par],))
        ctx.barrier()
        stk.close()
        pstk.close()

    def phase_ffn(tag, src, dst):
        ctx.new_phase_sems()
        stk, pstk = ExitStack(), ExitStack()
        NTI = NTOK // FT
        NGR = FT // 512 if FT >= 512 else 1
        GW = min(512, FT)
        NTG = FT // 128
        ln_2 = [sbuf(f"{tag}_ln{i}", [128, HC * FT], BF16, stk)
                for i in range(2)]
        bLn_2 = [ctx.buf(f"{tag}_ln{i}") for i in range(2)]
        routed = sbuf(f"{tag}_rt", [128, HC * FT], BF16, stk)
        bRt = ctx.buf(f"{tag}_rt")
        c_all = sbuf(f"{tag}_ca", [8, FT], BF16, stk)
        bCa = ctx.buf(f"{tag}_ca")
        glog = sbuf(f"{tag}_gl", [128, 2 * E + 2], F32, stk)
        bGl = ctx.buf(f"{tag}_gl")
        gmsk = sbuf(f"{tag}_gm", [128, E], BF16, stk)
        bGm = ctx.buf(f"{tag}_gm")
        xs = [sbuf(f"{tag}_xs{i}", [128, HC * FT], BF16, stk) for i in range(2)]
        bXs = [ctx.buf(f"{tag}_xs{i}") for i in range(2)]
        act_t = [sbuf(f"{tag}_at{i}", [128, IC2 * GW], BF16, stk)
                 for i in range(2)]
        bAt = [ctx.buf(f"{tag}_at{i}") for i in range(2)]
        sg_sb = [sbuf(f"{tag}_sg{i}", [128, GW], BF16, stk) for i in range(2)]
        bSg = [ctx.buf(f"{tag}_sg{i}") for i in range(2)]
        wg = [sbuf(f"{tag}_wg{i}", [128, HC * IH], BF16, stk) for i in range(2)]
        wu = [sbuf(f"{tag}_wu{i}", [128, HC * IH], BF16, stk) for i in range(2)]
        wd = [sbuf(f"{tag}_wd{i}", [128, IC2 * H], BF16, stk) for i in range(2)]
        bWg = [ctx.buf(f"{tag}_wg{i}") for i in range(2)]
        bWu = [ctx.buf(f"{tag}_wu{i}") for i in range(2)]
        bWd = [ctx.buf(f"{tag}_wd{i}") for i in range(2)]
        xo1 = sbuf(f"{tag}_xo", [128, HC * GW], F32, stk)
        xo = [xo1, xo1]
        bXo1 = ctx.buf(f"{tag}_xo")
        bXo = [bXo1, bXo1]

        ps_ct = psum(f"{tag}_pct", [8, 128], BF16, pstk)
        bPct = ctx.buf(f"{tag}_pct")
        ps_g2 = [psum(f"{tag}_pg{i}", [128, GW], F32, pstk) for i in range(2)]
        bPg2 = [ctx.buf(f"{tag}_pg{i}") for i in range(2)]
        ps_u = psum(f"{tag}_pu", [128, GW], F32, pstk)
        bPu = ctx.buf(f"{tag}_pu")
        ps_d2 = [psum(f"{tag}_pd{i}", [128, GW], F32, pstk) for i in range(2)]
        bPd2 = [ctx.buf(f"{tag}_pd{i}") for i in range(2)]
        kit = LNKit(f"{tag}_lnk", FT if FT <= 512 else 512, stk, pstk)
        ps_rep, bPrep = kit.pb, kit.b_pb
        Qw = [ctx.queue(tag + "_qw0"), ctx.queue(tag + "_qw1")]
        Qxr = [ctx.queue(tag + "_qxr0"), ctx.queue(tag + "_qxr1")]
        Qxo = [ctx.queue(tag + "_qxo0"), ctx.queue(tag + "_qxo1")]
        gw32 = sbuf(f"{tag}_gw32", [128, HC * E], F32, stk)
        bGw32 = ctx.buf(f"{tag}_gw32")
        gsum = sbuf(f"{tag}_gsum", [128, E + 1], F32, stk)
        bGsum = ctx.buf(f"{tag}_gsum")
        glc = sbuf(f"{tag}_glc", [128, 2], F32, stk)
        bGlc = ctx.buf(f"{tag}_glc")
        sp.dma(lambda e: e.dma_start(
            gw32[:, :], gate_w.rearrange("(c p) n -> p c n", p=128)),
            ctx.queue(tag + "_gw"), outs=(bGw32,))
        def fcs(e):
            for c in range(HC):
                i = e.matmul(kit.pa[0:1, 0:E], g_ocf[:, :],
                             gw32[:, c * E:(c + 1) * E],
                             start=(c == 0), stop=(c == HC - 1))
            return i
        pe.op(fcs, ins=(bGw32, bC), outs=(kit.b_pa,))
        dve.op(lambda e: e.tensor_copy(gsum[0:1, E:E + 1].rearrange(
            "a b -> a b") if False else gsum[0:1, 0:E], kit.pa[0:1, 0:E]),
            ins=(kit.b_pa,), outs=(bGsum,))
        pe.op(lambda e: e.matmul(kit.pa[:, 0:E], g_orf[:, :], gsum[0:1, 0:E],
                                 start=True, stop=True),
              ins=(bGsum, bC), outs=(kit.b_pa,))
        dve.op(lambda e: e.tensor_copy(gsum[:, 0:E], kit.pa[:, 0:E]),
               ins=(kit.b_pa,), outs=(bGsum,))

        e_order = [8] + list(range(8))

        for ti in range(NTI):
            col0 = ti * FT
            ln_t, bLn = ln_2[ti % 2], bLn_2[ti % 2]
            for cg in range(cdiv(FT, 512)):
                w = min(FT, (cg + 1) * 512) - cg * 512
                emit_ln_into(kit, src, col0 + cg * 512, ln_t, bLn, FT,
                             cg * 512, w)
                for tgl in range(w // 128):
                    tg = (cg * 512) // 128 + tgl
                    def fg(e, tgl=tgl, w=w):
                        e.matmul(kit.pa[:, 0:1],
                                 kit.rr[0:1, tgl * 128:(tgl + 1) * 128],
                                 g_orf[0:1, 0:1], start=True, stop=True)
                        e.matmul(kit.pa[:, 1:2],
                                 kit.rb[0:1, tgl * 128:(tgl + 1) * 128],
                                 g_orf[0:1, 0:1], start=True, stop=True)
                        for c in range(HC):
                            i = e.matmul(kit.pa[:, 2:2 + E],
                                         kit.xt[:, c * w + tgl * 128:
                                                c * w + tgl * 128 + 128],
                                         gw32[:, c * E:(c + 1) * E],
                                         start=(c == 0), stop=(c == HC - 1))
                        return i
                    pe.op(fg, ins=(kit.b_rr, kit.b_rb, kit.b_xt, bGw32, bC),
                          outs=(kit.b_pa,))
                    dve.op(lambda e: e.tensor_copy(glc[:, :], kit.pa[:, 0:2]),
                           ins=(kit.b_pa,), outs=(bGlc,))
                    dve.op(lambda e: e.tensor_scalar(
                        glog[:, 0:E], kit.pa[:, 2:2 + E], glc[:, 0:1], None,
                        op0=ALU.mult),
                        ins=(kit.b_pa, bGlc), outs=(bGl,))
                    dve.op(lambda e: e.tensor_scalar(
                        glog[:, E + 2:2 * E + 2], gsum[:, 0:E], glc[:, 1:2],
                        None, op0=ALU.mult),
                        ins=(bGsum, bGlc), outs=(bGl,))
                    dve.op(lambda e: e.tensor_sub(
                        glog[:, 0:E], glog[:, 0:E], glog[:, E + 2:2 * E + 2]),
                        ins=(bGl,), outs=(bGl,))
                    dve.op(lambda e: e.reduce_max(glog[:, E:E + 1],
                                                  glog[:, 0:E], axis=AX.X),
                           ins=(bGl,), outs=(bGl,))
                    act.op(lambda e: e.activation(glog[:, E + 1:E + 2],
                                                  glog[:, E:E + 1], AF.Sigmoid),
                           ins=(bGl,), outs=(bGl,))
                    dve.op(lambda e: e.tensor_scalar(
                        gmsk[:, :], glog[:, 0:E], glog[:, E:E + 1], None,
                        op0=ALU.is_equal), ins=(bGl,), outs=(bGm,))
                    dve.op(lambda e: e.tensor_scalar(
                        gmsk[:, :], gmsk[:, :], glog[:, E + 1:E + 2], None,
                        op0=ALU.mult), ins=(bGm, bGl), outs=(bGm,))
                    pe.op(lambda e: e.transpose(ps_ct[:, :], gmsk[:, :],
                                                g_id[:, :]),
                          ins=(bGm, bC), outs=(bPct,))
                    dve.op(lambda e, tg=tg: e.tensor_copy(
                        c_all[:, tg * 128:(tg + 1) * 128], ps_ct[:, :]),
                        ins=(bPct,), outs=(bCa,))
            def emit_xs(e_idx, epar, ln_t=ln_t, bLn=bLn):
                for grp in range(NGR):
                    gn0 = grp * GW
                    pe.op(lambda e, e_idx=e_idx, gn0=gn0: e.matmul(
                        ps_rep[:, 0:GW],
                        g_esel[:, e_idx * 128:(e_idx + 1) * 128],
                        c_all[:, gn0:gn0 + GW], start=True, stop=True),
                        ins=(bCa, bC), outs=(bPrep,))
                    def fxs(e, gn0=gn0, epar=epar, ln_t=ln_t):
                        for c in range(HC):
                            i = e.tensor_mul(
                                xs[epar][:, c * FT + gn0:c * FT + gn0 + GW],
                                ln_t[:, c * FT + gn0:c * FT + gn0 + GW],
                                ps_rep[:, 0:GW])
                        return i
                    dve.op(fxs, ins=(bLn, bPrep), outs=(bXs[epar],))
            for ei, e_idx in enumerate(e_order):
                if ei + 1 < len(e_order):
                    emit_xs(e_order[ei + 1], (ei + 1) % 2)
                epar = ei % 2
                for half in range(SPL):
                    wpar = (ei * SPL + half) % 2
                    if e_idx == 8:
                        srcs = (sh_gate[:, half * IH:(half + 1) * IH],
                                sh_up[:, half * IH:(half + 1) * IH],
                                sh_down[half * IH:(half + 1) * IH, :])
                    else:
                        srcs = (rt_gate[e_idx, :, half * IH:(half + 1) * IH],
                                rt_up[e_idx, :, half * IH:(half + 1) * IH],
                                rt_down[e_idx, half * IH:(half + 1) * IH, :])
                    pool.dma(lambda e, s=srcs[0], wpar=wpar: e.dma_start(
                        wg[wpar][:, :], s.rearrange("(c p) n -> p c n", p=128)),
                        Qw[wpar], outs=(bWg[wpar],))
                    pool.dma(lambda e, s=srcs[1], wpar=wpar: e.dma_start(
                        wu[wpar][:, :], s.rearrange("(c p) n -> p c n", p=128)),
                        Qw[wpar], outs=(bWu[wpar],))
                    pool.dma(lambda e, s=srcs[2], wpar=wpar: e.dma_start(
                        wd[wpar][:, :], s.rearrange("(c p) n -> p c n", p=128)),
                        Qw[wpar], outs=(bWd[wpar],))
                    m = (Qw[wpar], Qw[wpar].count)
                    bWg[wpar].w = bWu[wpar].w = bWd[wpar].w = m
                    for grp in range(NGR):
                        gn0 = grp * GW
                        use_ln = (e_idx == 8)
                        apar = (ei * SPL + half) % 2
                        for ic in range(IC2):
                            sgp = ic % 2
                            gp = ic % 2
                            ps_g, bPg = ps_g2[gp], bPg2[gp]
                            def fgm(e, ic=ic, gn0=gn0, wpar=wpar, epar=epar,
                                    use_ln=use_ln, ps_g=ps_g, ln_t=ln_t):
                                for c in range(HC):
                                    rhs = (ln_t[:, c * FT + gn0:c * FT + gn0 + GW]
                                           if use_ln else
                                           xs[epar][:, c * FT + gn0:
                                                    c * FT + gn0 + GW])
                                    i = e.matmul(
                                        ps_g[:, :],
                                        wg[wpar][:, c * IH + ic * 128:
                                                 c * IH + (ic + 1) * 128],
                                        rhs, start=(c == 0), stop=(c == HC - 1))
                                return i
                            def fum(e, ic=ic, gn0=gn0, wpar=wpar, epar=epar,
                                    use_ln=use_ln, ln_t=ln_t):
                                for c in range(HC):
                                    rhs = (ln_t[:, c * FT + gn0:c * FT + gn0 + GW]
                                           if use_ln else
                                           xs[epar][:, c * FT + gn0:
                                                    c * FT + gn0 + GW])
                                    i = e.matmul(
                                        ps_u[:, :],
                                        wu[wpar][:, c * IH + ic * 128:
                                                 c * IH + (ic + 1) * 128],
                                        rhs, start=(c == 0), stop=(c == HC - 1))
                                return i
                            ins_x = (bWg[wpar], bLn) if use_ln else (bWg[wpar], bXs[epar])
                            ins_xu = (bWu[wpar], bLn) if use_ln else (bWu[wpar], bXs[epar])
                            pe.op(fgm, ins=ins_x, outs=(bPg,))
                            pe.op(fum, ins=ins_xu, outs=(bPu,))
                            act.op(lambda e, sgp=sgp, ps_g=ps_g: e.activation(
                                sg_sb[sgp][:, :], ps_g[:, :], AF.Silu),
                                ins=(bPg,), outs=(bSg[sgp],))
                            dve.op(lambda e, ic=ic, sgp=sgp, apar=apar:
                                   e.tensor_mul(
                                       act_t[apar][:, ic * GW:(ic + 1) * GW],
                                       sg_sb[sgp][:, :], ps_u[:, :]),
                                   ins=(bSg[sgp], bPu), outs=(bAt[apar],))
                        first = (ei == 0 and half == 0)
                        for hc in range(HC):
                            dp = hc % 2
                            ps_d, bPd = ps_d2[dp], bPd2[dp]
                            def fdm(e, hc=hc, wpar=wpar, apar=apar, ps_d=ps_d):
                                for ic in range(IC2):
                                    i = e.matmul(
                                        ps_d[:, :],
                                        wd[wpar][:, ic * H + hc * 128:
                                                 ic * H + (hc + 1) * 128],
                                        act_t[apar][:, ic * GW:(ic + 1) * GW],
                                        start=(ic == 0), stop=(ic == IC2 - 1))
                                return i
                            pe.op(fdm, ins=(bWd[wpar], bAt[apar]), outs=(bPd,))
                            if first:
                                dve.op(lambda e, hc=hc, gn0=gn0, ps_d=ps_d:
                                       e.tensor_copy(
                                           routed[:, hc * FT + gn0:
                                                  hc * FT + gn0 + GW],
                                           ps_d[:, :]),
                                       ins=(bPd,), outs=(bRt,))
                            else:
                                def fac(e, hc=hc, gn0=gn0, ps_d=ps_d):
                                    return e.tensor_add(
                                        routed[:, hc * FT + gn0:hc * FT + gn0 + GW],
                                        ps_d[:, :],
                                        routed[:, hc * FT + gn0:hc * FT + gn0 + GW])
                                dve.op(fac, ins=(bPd, bRt), outs=(bRt,))
            for grp in range(NGR):
                gn0 = grp * GW
                par = grp % 2
                sp.dma(lambda e, gn0=gn0, col0=col0, par=par: e.dma_start(
                    xo[par][:, 0:HC * GW],
                    src[:, col0 + gn0:col0 + gn0 + GW].rearrange(
                        "(c p) n -> p c n", p=128)),
                    Qxr[0], outs=(bXo[par],))
                for hc in range(HC):
                    def fr(e, hc=hc, gn0=gn0, par=par):
                        return e.tensor_add(
                            xo[par][:, hc * GW:(hc + 1) * GW],
                            xo[par][:, hc * GW:(hc + 1) * GW],
                            routed[:, hc * FT + gn0:hc * FT + gn0 + GW])
                    dve.op(fr, ins=(bXo[par], bRt), outs=(bXo[par],))
                sp.dma(lambda e, gn0=gn0, par=par, col0=col0: e.dma_start(
                    dst[:, col0 + gn0:col0 + gn0 + GW].rearrange(
                        "(c p) n -> p c n", p=128),
                    xo[par][:, :]), Qxo[par], ins=(bXo[par],))
        ctx.barrier()
        stk.close()
        pstk.close()

    def emit_ln_into(kit, src_dram, col0, ln_sb, b_ln, FTT, off, w):
        xt, xb, sq = kit.xt, kit.xb, kit.sq
        TT = kit.TT
        assert w <= TT
        sp.dma(lambda e: e.dma_start(
            xt[:, 0:HC * w],
            src_dram[:, col0:col0 + w].rearrange("(c p) n -> p c n", p=128)),
            kit.qx, outs=(kit.b_xt,))
        act.op(lambda e: e.copy(xb[:, 0:HC * w], xt[:, 0:HC * w]),
               ins=(kit.b_xt,), outs=(kit.b_xb,))
        act.op(lambda e: e.activation(sq[:, 0:HC * w], xb[:, 0:HC * w],
                                      AF.Square),
               ins=(kit.b_xb,), outs=(kit.b_sq,))
        rH = 1.0 / H
        def fm(e):
            for c in range(HC):
                i = e.matmul(kit.pm[0:1, 0:w], g_oc[:, :],
                             xb[:, c * w:(c + 1) * w],
                             start=(c == 0), stop=(c == HC - 1))
            return i
        def fs(e):
            for c in range(HC):
                i = e.matmul(kit.ps[0:1, 0:w], g_oc[:, :],
                             sq[:, c * w:(c + 1) * w],
                             start=(c == 0), stop=(c == HC - 1))
            return i
        pe.op(fm, ins=(kit.b_xb, bC), outs=(kit.b_pm,))
        pe.op(fs, ins=(kit.b_sq, bC), outs=(kit.b_ps,))
        dve.op(lambda e: e.tensor_scalar(kit.rm[0:1, 0:w], kit.pm[0:1, 0:w],
                                         rH, None, op0=ALU.mult),
               ins=(kit.b_pm,), outs=(kit.b_rm,))
        dve.op(lambda e: e.tensor_scalar(kit.rv[0:1, 0:w], kit.ps[0:1, 0:w],
                                         rH, None, op0=ALU.mult),
               ins=(kit.b_ps,), outs=(kit.b_rv,))
        dve.op(lambda e: e.tensor_mul(kit.rb[0:1, 0:w], kit.rm[0:1, 0:w],
                                      kit.rm[0:1, 0:w]),
               ins=(kit.b_rm,), outs=(kit.b_rb,))
        dve.op(lambda e: e.tensor_sub(kit.rv[0:1, 0:w], kit.rv[0:1, 0:w],
                                      kit.rb[0:1, 0:w]),
               ins=(kit.b_rv, kit.b_rb), outs=(kit.b_rv,))
        dve.op(lambda e: e.tensor_scalar(kit.rv[0:1, 0:w], kit.rv[0:1, 0:w],
                                         1e-5, None, op0=ALU.add),
               ins=(kit.b_rv,), outs=(kit.b_rv,))
        act.op(lambda e: e.activation(kit.rr[0:1, 0:w], kit.rv[0:1, 0:w],
                                      AF.Ln),
               ins=(kit.b_rv,), outs=(kit.b_rr,))
        act.op(lambda e: e.activation(kit.rr[0:1, 0:w], kit.rr[0:1, 0:w],
                                      AF.Exp, scale=-0.5),
               ins=(kit.b_rr,), outs=(kit.b_rr,))
        dve.op(lambda e: e.tensor_mul(kit.rb[0:1, 0:w], kit.rm[0:1, 0:w],
                                      kit.rr[0:1, 0:w]),
               ins=(kit.b_rm, kit.b_rr), outs=(kit.b_rb,))
        pe.op(lambda e: e.matmul(kit.pa[:, 0:w], g_orf[:, :], kit.rr[0:1, 0:w],
                                 start=True, stop=True),
              ins=(kit.b_rr, bC), outs=(kit.b_pa,))
        pe.op(lambda e: e.matmul(kit.pb[:, 0:w], g_orf[:, :], kit.rb[0:1, 0:w],
                                 start=True, stop=True),
              ins=(kit.b_rb, bC), outs=(kit.b_pb,))
        for c in range(HC):
            dve.op(lambda e, c=c: e.tensor_mul(
                sq[:, c * w:(c + 1) * w], xt[:, c * w:(c + 1) * w],
                kit.pa[:, 0:w]),
                ins=(kit.b_xt, kit.b_pa), outs=(kit.b_sq,))
            dve.op(lambda e, c=c: e.tensor_sub(
                ln_sb[:, c * FTT + off:c * FTT + off + w],
                sq[:, c * w:(c + 1) * w], kit.pb[:, 0:w]),
                ins=(kit.b_sq, kit.b_pb), outs=(b_ln,))

    def phase_p5():
        ctx.new_phase_sems()
        stk, pstk = ExitStack(), ExitStack()
        wq_sb = sbuf("p5_wq", [128, HC * H], BF16, stk)
        wk_sb = sbuf("p5_wk", [128, HC * H], BF16, stk)
        wv_sb = sbuf("p5_wv", [128, HC * H], BF16, stk)
        bWq, bWk, bWv = (ctx.buf("p5_wq"), ctx.buf("p5_wk"), ctx.buf("p5_wv"))
        Qpw = ctx.queue("p5_w")
        pool.dma(lambda e: e.dma_start(
            wq_sb[:, :], wq.rearrange("(c p) n -> p c n", p=128)), Qpw,
            outs=(bWq,))
        pool.dma(lambda e: e.dma_start(
            wk_sb[:, :], wk.rearrange("(c p) n -> p c n", p=128)), Qpw,
            outs=(bWk,))
        pool.dma(lambda e: e.dma_start(
            wv_sb[:, :], wv.rearrange("(c p) n -> p c n", p=128)), Qpw,
            outs=(bWv,))
        m = (Qpw, Qpw.count)
        bWq.w = bWk.w = bWv.w = m
        Qs5 = [ctx.queue("p5_s0"), ctx.queue("p5_s1")]
        kit = LNKit("p5ln", GS, stk, pstk)
        ln = [sbuf(f"p5_ln{i}", [128, HC * GS], BF16, stk) for i in range(2)]
        bL = [ctx.buf(f"p5_ln{i}") for i in range(2)]
        pq = [psum(f"p5_pq{i}", [128, GS], F32, pstk) for i in range(2)]
        bQ = [ctx.buf(f"p5_pq{i}") for i in range(2)]
        qs = [sbuf(f"p5_qs{i}", [128, GS], BF16, stk) for i in range(2)]
        bqs = [ctx.buf(f"p5_qs{i}") for i in range(2)]
        rot = sbuf("p5_rot", [128, GS], BF16, stk)
        brot = ctx.buf("p5_rot")
        rp2 = sbuf("p5_rp2", [128, GS], BF16, stk)
        brp2 = ctx.buf("p5_rp2")

        def rope(qsb, bqsb, pos0, w0):
            cw = GS - w0
            cc0 = pos0 + w0 - T // 2
            def fr1(e):
                e.tensor_scalar(rot[0:32, w0:GS], qsb[32:64, w0:GS], -1.0,
                                None, op0=ALU.mult)
                e.tensor_copy(rot[32:64, w0:GS], qsb[0:32, w0:GS])
                e.tensor_scalar(rot[64:96, w0:GS], qsb[96:128, w0:GS], -1.0,
                                None, op0=ALU.mult)
                return e.tensor_copy(rot[96:128, w0:GS], qsb[64:96, w0:GS])
            dve.op(fr1, ins=(bqsb,), outs=(brot,))
            dve.op(lambda e: e.tensor_mul(rp2[:, w0:GS], rot[:, w0:GS],
                                          g_sin[:, cc0:cc0 + cw]),
                   ins=(brot, bC), outs=(brp2,))
            dve.op(lambda e: e.tensor_mul(rot[:, w0:GS], qsb[:, w0:GS],
                                          g_cos[:, cc0:cc0 + cw]),
                   ins=(bqsb, bC), outs=(brot,))
            dve.op(lambda e: e.tensor_add(qsb[:, w0:GS], rot[:, w0:GS],
                                          rp2[:, w0:GS]),
                   ins=(brot, brp2), outs=(bqsb,))

        for ti in range(NTOK // GS):
            par = ti % 2
            emit_ln(kit, X, ti * GS, ln[par], bL[par])
            pos0 = (ti * GS) % T
            ropew = None
            if pos0 + GS > T // 2:
                ropew = max(0, T // 2 - pos0)
            for which, wsb, bW, dstD, has_b in (
                    ("q", wq_sb, bWq, Qb, True), ("k", wk_sb, bWk, Kb, False)):
                for jc in range(HC):
                    pp = jc % 2
                    def fm(e, jc=jc, par=par, pp=pp, wsb=wsb):
                        for c in range(HC):
                            i = e.matmul(pq[pp][:, :],
                                         wsb[:, c * H + jc * 128:
                                             c * H + (jc + 1) * 128],
                                         ln[par][:, c * GS:(c + 1) * GS],
                                         start=(c == 0), stop=(c == HC - 1))
                        return i
                    pe.op(fm, ins=(bW, bL[par]), outs=(bQ[pp],))
                    if has_b:
                        def fe(e, jc=jc, pp=pp):
                            return e.tensor_scalar(
                                qs[pp][:, :], pq[pp][:, :],
                                g_bq[:, jc:jc + 1], scal,
                                op0=ALU.add, op1=ALU.mult)
                        dve.op(fe, ins=(bQ[pp], bC), outs=(bqs[pp],))
                    else:
                        act.op(lambda e, pp=pp: e.copy(qs[pp][:, :],
                                                       pq[pp][:, :]),
                               ins=(bQ[pp],), outs=(bqs[pp],))
                    if ropew is not None:
                        rope(qs[pp], bqs[pp], pos0, ropew)
                    sp.dma(lambda e, jc=jc, ti=ti, pp=pp, dstD=dstD:
                           e.dma_start(
                               dstD[jc * 128:(jc + 1) * 128,
                                    ti * GS:(ti + 1) * GS], qs[pp][:, :]),
                           Qs5[pp], ins=(bqs[pp],))
            for tg in range(GS // 128):
                g_glob = (ti * GS) // 128 + tg
                for nh2 in range(cdiv(H, 512)):
                    jn0 = nh2 * 512
                    jw = min(H, jn0 + 512) - jn0
                    pp = tg % 2
                    def fv(e, tg=tg, jn0=jn0, jw=jw, par=par, pp=pp):
                        for c in range(HC):
                            e.matmul(pq[pp][:, 0:jw],
                                     ln[par][:, c * GS + tg * 128:
                                             c * GS + (tg + 1) * 128],
                                     wv_sb[:, c * H + jn0:c * H + jn0 + jw],
                                     start=(c == 0), stop=False)
                        return e.matmul(pq[pp][:, 0:jw], g_or[0:1, :],
                                        g_bv[0:1, jn0:jn0 + jw],
                                        start=False, stop=True)
                    pe.op(fv, ins=(bWv, bL[par], bC), outs=(bQ[pp],))
                    act.op(lambda e, pp=pp, jw=jw: e.copy(
                        qs[pp][:, 0:jw], pq[pp][:, 0:jw]),
                        ins=(bQ[pp],), outs=(bqs[pp],))
                    for hh in range(jw // HS):
                        hA = jn0 // HS + hh
                        sp.dma(lambda e, hA=hA, g_glob=g_glob, pp=pp, hh=hh:
                               e.dma_start(Vb[hA, g_glob],
                                           qs[pp][:, hh * HS:(hh + 1) * HS]),
                               Qs5[pp], ins=(bqs[pp],))
        ctx.barrier()
        stk.close()
        pstk.close()

    def drive(gens, W=2):
        from collections import deque
        act_ = deque()
        it = iter(gens)
        while True:
            while len(act_) < W:
                g = next(it, None)
                if g is None:
                    break
                act_.append(g)
            if not act_:
                break
            g = act_.popleft()
            try:
                next(g)
                act_.append(g)
            except StopIteration:
                pass

    def phase_p6():
        ctx.new_phase_sems()
        stk, pstk = ExitStack(), ExitStack()
        NKB = T // 128
        NQB = cdiv(T, 512)
        QB = min(512, T)
        HS1 = HS + 1
        NPAR = 4
        def mk2(nm, shape, dtyp):
            return ([sbuf(f"{nm}{i}", shape, dtyp, stk) for i in range(NPAR)],
                    [ctx.buf(f"{nm}{i}") for i in range(NPAR)])
        Qlq = [ctx.queue(f"p6_lq{i}") for i in range(NPAR)]
        Qlk = [ctx.queue(f"p6_lk{i}") for i in range(NPAR)]
        Qlv = [ctx.queue(f"p6_lv{i}") for i in range(NPAR)]
        Qao = [ctx.queue(f"p6_ao{i}") for i in range(NPAR)]
        q2, bq2 = mk2("p6_q2", [66, T], BF16)
        k2, bk2 = mk2("p6_k2", [66, T], BF16)
        vtm, bvtm = mk2("p6_vtm", [128, NKB * HS1], BF16)
        sqs, bsqs = mk2("p6_sqs", [64, T], BF16)
        qn, bqn = mk2("p6_qn", [1, T], F32)
        nC, bnC = mk2("p6_nC", [1, T], F32)
        nCh, bnCh = mk2("p6_nCh", [1, T], BF16)
        mk2v, bmk2 = mk2("p6_mk2", [1, 2], F32)
        attb, battb = mk2("p6_attb", [128, QB], BF16)
        oev, boev = mk2("p6_oev", [64, QB], BF16)
        osb, bosb = mk2("p6_osb", [64, QB], BF16)
        den, bden = mk2("p6_den", [1, QB], BF16)
        recb, brecb = mk2("p6_recb", [64, QB], F32)
        ps2 = [psum(f"p6_ps2{i}", [128, QB], F32, pstk) for i in range(4)]
        bps2 = [ctx.buf(f"p6_ps2{i}") for i in range(4)]
        pso2 = [psum(f"p6_pso{i}", [65, QB], F32, pstk) for i in range(2)]
        bpso2 = [ctx.buf(f"p6_pso{i}") for i in range(2)]
        psrm = [psum(f"p6_psrm{i}", [65, 512], F32, pstk) for i in range(2)]
        bpsrm = [ctx.buf(f"p6_psrm{i}") for i in range(2)]
        for p in range(NPAR):
            dve.op(lambda e, p=p: e.memset(
                vtm[p][:, HS:NKB * HS1:HS1], 1.0), outs=(bvtm[p],))
            dve.op(lambda e, p=p: e.memset(k2[p][64:66, :], 1.0),
                   outs=(bk2[p],))
            dve.op(lambda e, p=p: e.memset(q2[p][64:66, :], 0.0),
                   outs=(bq2[p],))

        def gen(it_idx, s, h):
            p = it_idx % NPAR
            slot = it_idx % 2
            c0 = s * T
            sp.dma(lambda e, h=h, c0=c0, p=p: e.dma_start(
                q2[p][0:64, :], Qb[h * HS:(h + 1) * HS, c0:c0 + T]),
                Qlq[p], outs=(bq2[p],))
            sp.dma(lambda e, h=h, c0=c0, p=p: e.dma_start(
                k2[p][0:64, :], Kb[h * HS:(h + 1) * HS, c0:c0 + T]),
                Qlk[p], outs=(bk2[p],))
            g0 = c0 // 128
            sp.dma(lambda e, h=h, g0=g0, p=p: e.dma_start(
                vtm[p][:, :].rearrange("q (g c) -> q g c", c=HS1)[:, :, 0:HS],
                Vb[h, g0:g0 + NKB].rearrange("g q c -> q g c")),
                Qlv[p], outs=(bvtm[p],))
            yield
            act.op(lambda e, p=p: e.activation(sqs[p][:, :], q2[p][0:64, :],
                                               AF.Square),
                   ins=(bq2[p],), outs=(bsqs[p],))
            for cg in range(T // 512):
                n0 = cg * 512
                pe.op(lambda e, n0=n0, p=p, slot=slot: e.matmul(
                    psrm[slot][64:65, :], g_oc[0:64, :], sqs[p][:, n0:n0 + 512],
                    start=True, stop=True),
                    ins=(bsqs[p], bC), outs=(bpsrm[slot],))
                dve.op(lambda e, n0=n0, p=p, slot=slot: e.tensor_copy(
                    qn[p][0:1, n0:n0 + 512], psrm[slot][64:65, :]),
                    ins=(bpsrm[slot],), outs=(bqn[p],))
            yield
            act.op(lambda e, p=p: e.activation(sqs[p][:, :], k2[p][0:64, :],
                                               AF.Square),
                   ins=(bk2[p],), outs=(bsqs[p],))
            for cg in range(T // 512):
                n0 = cg * 512
                pe.op(lambda e, n0=n0, p=p, slot=slot: e.matmul(
                    psrm[slot][64:65, :], g_oc[0:64, :], sqs[p][:, n0:n0 + 512],
                    start=True, stop=True),
                    ins=(bsqs[p], bC), outs=(bpsrm[slot],))
                dve.op(lambda e, n0=n0, p=p, slot=slot: e.tensor_copy(
                    nC[p][0:1, n0:n0 + 512], psrm[slot][64:65, :]),
                    ins=(bpsrm[slot],), outs=(bnC[p],))
            yield
            dve.op(lambda e, p=p: e.reduce_max(mk2v[p][:, 0:1],
                                               nC[p][0:1, :], axis=AX.X),
                   ins=(bnC[p],), outs=(bmk2[p],))
            act.op(lambda e, p=p: e.activation(mk2v[p][:, 1:2],
                                               mk2v[p][:, 0:1], AF.Ln),
                   ins=(bmk2[p],), outs=(bmk2[p],))
            act.op(lambda e, p=p: e.activation(nC[p][0:1, :], qn[p][0:1, :],
                                               AF.Ln),
                   ins=(bqn[p],), outs=(bnC[p],))
            dve.op(lambda e, p=p: e.tensor_scalar(
                nC[p][0:1, :], nC[p][0:1, :], mk2v[p][0:1, 1:2], None,
                op0=ALU.add),
                ins=(bnC[p], bmk2[p]), outs=(bnC[p],))
            act.op(lambda e, p=p: e.activation(nC[p][0:1, :], nC[p][0:1, :],
                                               AF.Exp, scale=0.5),
                   ins=(bnC[p],), outs=(bnC[p],))
            dve.op(lambda e, p=p: e.tensor_scalar(
                nC[p][0:1, :], nC[p][0:1, :], -1.0, None, op0=ALU.mult),
                ins=(bnC[p],), outs=(bnC[p],))
            act.op(lambda e, p=p: e.copy(nCh[p][:, :], nC[p][0:1, :]),
                   ins=(bnC[p],), outs=(bnCh[p],))
            for cg in range(T // 512):
                n0 = cg * 512
                pb = slot * 2
                pe.op(lambda e, n0=n0, p=p, pb=pb: e.matmul(
                    ps2[pb][:, :], g_e64[0:1, :], nCh[p][0:1, n0:n0 + 512],
                    start=True, stop=True),
                    ins=(bnCh[p], bC), outs=(bps2[pb],))
                dve.op(lambda e, n0=n0, p=p, pb=pb: e.tensor_copy(
                    q2[p][64:65, n0:n0 + 512], ps2[pb][64:65, :]),
                    ins=(bps2[pb],), outs=(bq2[p],))
            yield
            for qb in range(NQB):
                qn0 = qb * QB
                pso, bpso = pso2[slot], bpso2[slot]
                for kb in range(NKB):
                    pb2 = slot * 2 + kb % 2
                    def f5(e, kb=kb, qn0=qn0, p=p, pb2=pb2):
                        return e.matmul(
                            ps2[pb2][:, :],
                            k2[p][:, kb * 128:(kb + 1) * 128],
                            q2[p][:, qn0:qn0 + QB], start=True, stop=True)
                    pe.op(f5, ins=(bk2[p], bq2[p]), outs=(bps2[pb2],))
                    act.op(lambda e, pb2=pb2: e.activation(
                        attb[pb2][:, :], ps2[pb2][:, :], AF.Exp),
                        ins=(bps2[pb2],), outs=(battb[pb2],))
                    yield
                    def f6(e, kb=kb, p=p, pb2=pb2, pso=pso):
                        return e.matmul(
                            pso[:, :], vtm[p][:, kb * HS1:(kb + 1) * HS1],
                            attb[pb2][:, :], start=(kb == 0),
                            stop=(kb == NKB - 1))
                    pe.op(f6, ins=(bvtm[p], battb[pb2]), outs=(bpso,))
                dve.op(lambda e, p=p, pso=pso: e.tensor_copy(
                    den[p][0:1, :], pso[64:65, :]),
                    ins=(bpso,), outs=(bden[p],))
                pe.op(lambda e, p=p, slot=slot: e.matmul(
                    psrm[slot][0:64, 0:QB], g_or[0:1, 0:64], den[p][0:1, :],
                    start=True, stop=True),
                    ins=(bden[p], bC), outs=(bpsrm[slot],))
                dve.op(lambda e, p=p, slot=slot: e.reciprocal(
                    recb[p][:, :], psrm[slot][0:64, 0:QB]),
                    ins=(bpsrm[slot],), outs=(brecb[p],))
                dve.op(lambda e, p=p, pso=pso: e.tensor_mul(
                    osb[p][:, :], pso[0:64, :], recb[p][:, :]),
                    ins=(bpso, brecb[p]), outs=(bosb[p],))
                sp.dma(lambda e, h=h, c0=c0, qn0=qn0, p=p: e.dma_start(
                    AO[h * HS:(h + 1) * HS, c0 + qn0:c0 + qn0 + QB],
                    osb[p][:, :]), Qao[p], ins=(bosb[p],))
                yield

        gens = []
        it_idx = 0
        for s in range(SEQ):
            for h in range(NH):
                gens.append(gen(it_idx, s, h))
                it_idx += 1
        drive(gens, W=2)
        ctx.barrier()
        stk.close()
        pstk.close()

    phase_fns = {
        "p1": phase_p1,
        "p2": phase_p2,
        "p3": lambda: phase_proj("p3", w_projT, x_in, X),
        "p4": lambda: phase_ffn("p4", X, X),
        "p5": phase_p5,
        "p6": phase_p6,
        "p7": lambda: phase_proj("p7", wo, X, X, bias_sb=g_bo),
        "p8": lambda: phase_ffn("p8", X, out_fm),
    }
    for ph in phases:
        phase_fns[ph]()

    ctx.barrier()

    with nc.Block() as block:
        @block.tensor
        def _(t):
            for f in ctx.pe.items:
                f(t)
        @block.vector
        def _(v):
            for f in ctx.dve.items:
                f(v)
        @block.scalar
        def _(s):
            for f in ctx.act.items:
                f(s)
        @block.gpsimd
        def _(g):
            for f in ctx.pool.items:
                f(g)
        @block.sync
        def _(sy):
            for f in ctx.sp.items:
                f(sy)

    esg.close()
    return nc


def make_in_maps(cfg, inputs, ncores=8):
    x = np.asarray(inputs["L0_attn_a_x"])
    B = x.shape[0]
    spc = B // ncores
    hostc = build_host_constants(cfg)
    w_attnT = np.ascontiguousarray(np.asarray(inputs["L0_attn_a_W_attn"]).T)
    w_projT = np.ascontiguousarray(np.asarray(inputs["L0_attn_a_W_proj"]).T)
    cosT = np.ascontiguousarray(np.asarray(inputs["L0_attn_b_cos"]).T)
    sinT = np.ascontiguousarray(np.asarray(inputs["L0_attn_b_sin"]).T)
    cos2 = np.ascontiguousarray(np.tile(cosT, (2, 1)))
    sin2 = np.ascontiguousarray(np.tile(sinT, (2, 1)))
    common = dict(
        w_attnT=w_attnT, w_projT=w_projT, cos2=cos2, sin2=sin2,
        wq=np.ascontiguousarray(inputs["L0_attn_b_wq"]),
        wk=np.ascontiguousarray(inputs["L0_attn_b_wk"]),
        wv=np.ascontiguousarray(inputs["L0_attn_b_wv"]),
        wo=np.ascontiguousarray(inputs["L0_attn_b_wo"]),
        bq=np.ascontiguousarray(
            np.asarray(inputs["L0_attn_b_bq"]).reshape(-1, 128).T),
        bv=np.ascontiguousarray(np.asarray(inputs["L0_attn_b_bv"])[:, None]),
        bo=np.ascontiguousarray(
            np.asarray(inputs["L0_attn_b_bo"]).reshape(-1, 128).T),
        gate_w=np.ascontiguousarray(inputs["L0_ffn_gate_w"]),
        sh_gate=np.ascontiguousarray(inputs["L0_ffn_sh_gate"]),
        sh_up=np.ascontiguousarray(inputs["L0_ffn_sh_up"]),
        sh_down=np.ascontiguousarray(inputs["L0_ffn_sh_down"]),
        rt_gate=np.ascontiguousarray(inputs["L0_ffn_rt_gate"]),
        rt_up=np.ascontiguousarray(inputs["L0_ffn_rt_up"]),
        rt_down=np.ascontiguousarray(inputs["L0_ffn_rt_down"]),
        **{k: np.ascontiguousarray(v) for k, v in hostc.items()},
    )
    in_maps = []
    for c in range(ncores):
        xc = x[c * spc:(c + 1) * spc]
        x_fm = np.ascontiguousarray(
            xc.transpose(2, 0, 1).reshape(cfg.H, spc * cfg.T))
        in_maps.append(dict(x_fm=x_fm.astype(np.float32), **common))
    return in_maps


def kernel(**inputs):
    cfg = REAL
    ncores = 8
    in_maps = make_in_maps(cfg, inputs, ncores)
    nc = build_program(cfg)
    res = run_bass_kernel_spmd(nc, in_maps, core_ids=list(range(ncores)))
    B = 32
    spc = B // ncores
    outs = []
    for c in range(ncores):
        o = res.results[c]["out_fm"]
        outs.append(o.reshape(cfg.H, spc, cfg.T).transpose(1, 2, 0))
    return np.ascontiguousarray(np.concatenate(outs, axis=0)).astype(np.float32)


# version 18
# speedup vs baseline: 1.2160x; 1.2160x over previous
import numpy as np
import ml_dtypes
from contextlib import ExitStack
import concourse.bass as bass
import concourse.mybir as mybir
from concourse.bass_utils import run_bass_kernel_spmd

F32 = mybir.dt.float32
BF16 = mybir.dt.bfloat16
AF = mybir.ActivationFunctionType
ALU = mybir.AluOpType
AX = mybir.AxisListType

NEG = -1.0e30


class Cfg:
    def __init__(self, T=2048, H=1024, I=2048, NH=16, E=8, SEQ=4):
        self.T, self.H, self.I, self.NH, self.E, self.SEQ = T, H, I, NH, E, SEQ
        self.HS = H // NH
        assert self.HS == 64 and H % 128 == 0 and I % 256 == 0
        self.NTOK = T * SEQ
        self.HC = H // 128
        self.ROPE_P = T // 2
        sets = []
        L = 4
        while L <= T:
            for i in range(T // L):
                sets.append((i * L, (i + 1) * L - 1))
            L *= 2
        self.sets = sets
        self.S = len(sets)
        self.SP1 = self.S + 1
        assert self.SP1 % 128 == 0
        self.GS = min(512, self.NTOK)
        self.FT = min(1024, self.NTOK)


REAL = Cfg()


def cdiv(a, b):
    return (a + b - 1) // b


def build_host_constants(cfg):
    T, SP1 = cfg.T, cfg.SP1
    r = np.array([b for (_, b) in cfg.sets], dtype=np.int64)
    t = np.arange(T)
    m = np.where(t[:, None] >= r[None, :], 0.0, NEG).astype(np.float32)
    mask_add = np.concatenate([m, np.zeros((T, 1), np.float32)], axis=1)
    mT = np.where(t[None, :] >= r[:, None], 0.0, NEG).astype(np.float32)
    maskT_add = np.concatenate([mT, np.full((1, T), NEG, np.float32)], axis=0)
    m4a = np.broadcast_to(((t % 4) >= 1).astype(np.float32), (64, T)).copy()
    m4b = np.broadcast_to(((t % 4) >= 2).astype(np.float32), (64, T)).copy()
    idn = np.eye(128, dtype=np.float32)
    ones_col = np.ones((128, 1), np.float32)
    ones_row = np.ones((1, 128), np.float32)
    esel = np.zeros((8, 8 * 128), np.float32)
    for e in range(8):
        esel[e, e * 128:(e + 1) * 128] = 1.0
    e64 = np.zeros((1, 128), np.float32)
    e64[0, 64] = 1.0
    return dict(c_mask_add=mask_add, c_maskT_add=maskT_add, c_m4a=m4a,
                c_m4b=m4b, c_idn=idn, c_ones_col=ones_col,
                c_ones_row=ones_row, c_esel=esel, c_e64=e64)


class SemC:
    __slots__ = ("h", "count")
    def __init__(self, h):
        self.h, self.count = h, 0


class Buf:
    __slots__ = ("name", "w", "readers")
    def __init__(self, name):
        self.name, self.w, self.readers = name, None, []


class Eng:
    def __init__(self, ctx, name):
        self.ctx, self.name = ctx, name
        self.items = []
        self.sem = None
        self.dsem = None
        self.waited = {}

    def wait(self, mark):
        self._deps_from([mark])

    def _deps_from(self, marks):
        best = {}
        for m in marks:
            if m is None:
                continue
            sc, val = m
            if val <= 0:
                continue
            prev = self.waited.get(id(sc))
            if prev is not None and prev[1] >= val:
                continue
            cur = best.get(id(sc))
            if cur is None or cur[1] < val:
                best[id(sc)] = (sc, val)
        for sc, val in best.values():
            self.waited[id(sc)] = (sc, val)
            self.items.append(lambda e, sc=sc, val=val: e.wait_ge(sc.h, val))

    def _deps(self, ins, outs):
        marks = [b.w for b in ins]
        for b in outs:
            marks.append(b.w)
            marks.extend(b.readers)
        self._deps_from(marks)

    def op(self, f, ins=(), outs=()):
        self._deps(ins, outs)
        self.sem.count += 1
        sc, cnt = self.sem, self.sem.count
        self.items.append(lambda e, f=f, sc=sc: f(e).then_inc(sc.h, 1))
        mark = (sc, cnt)
        for b in outs:
            b.w, b.readers = mark, []
        for b in ins:
            b.readers.append(mark)

    def dma(self, f, q, ins=(), outs=()):
        self._deps(ins, outs)
        q.count += 16
        cnt = q.count
        self.items.append(lambda e, f=f, q=q: f(e).then_inc(q.h, 16))
        mark = (q, cnt)
        for b in outs:
            b.w, b.readers = mark, []
        for b in ins:
            b.readers.append(mark)


class Ctx:
    def __init__(self, nc):
        self.nc = nc
        self.sem_guards = []
        self.all_sems = []
        self.bufs = {}
        self.pe = Eng(self, "pe")
        self.dve = Eng(self, "dve")
        self.act = Eng(self, "act")
        self.pool = Eng(self, "pool")
        self.sp = Eng(self, "sp")
        self.engines = [self.pe, self.dve, self.act, self.pool, self.sp]
        self.nsem = 0
        self._init_engine_sems = False

    def new_sem(self):
        cm = self.nc.semaphore(f"s{self.nsem}")
        self.nsem += 1
        h = cm.__enter__()
        self.sem_guards.append(cm)
        sc = SemC(h)
        self.all_sems.append(sc)
        return sc

    def new_phase_sems(self):
        if not self._init_engine_sems:
            for e in self.engines:
                e.sem = self.new_sem()
            self._init_engine_sems = True

    def queue(self, name):
        q = self.bufs.get("##q_" + name)
        if q is None:
            q = self.new_sem()
            self.bufs["##q_" + name] = q
        return q

    def buf(self, name):
        b = self.bufs.get(name)
        if b is None:
            b = Buf(name)
            self.bufs[name] = b
        return b

    def barrier(self):
        for e in self.engines:
            for sc in self.all_sems:
                if sc.count > 0:
                    e.wait((sc, sc.count))


def build_program(cfg, phases=("p1", "p2", "p3", "p4", "p5", "p6", "p7", "p8")):
    nc = bass.Bass("TRN2", target_bir_lowering=False)
    T, H, I, NH, E, SEQ = cfg.T, cfg.H, cfg.I, cfg.NH, cfg.E, cfg.SEQ
    HC, NTOK, SP1, GS, FT = cfg.HC, cfg.NTOK, cfg.SP1, cfg.GS, cfg.FT
    HS = cfg.HS
    SPL = I // 512 if I >= 1024 else 2
    IH = I // SPL
    IC2 = IH // 128
    scal = 0.125

    dt = nc.dram_tensor
    x_in = dt("x_fm", [H, NTOK], F32, kind="ExternalInput").ap()
    out_fm = dt("out_fm", [H, NTOK], F32, kind="ExternalOutput").ap()
    w_attnT = dt("w_attnT", [H, 3 * H], F32, kind="ExternalInput").ap()
    w_projT = dt("w_projT", [H, H], F32, kind="ExternalInput").ap()
    cos2 = dt("cos2", [128, cfg.ROPE_P], F32, kind="ExternalInput").ap()
    sin2 = dt("sin2", [128, cfg.ROPE_P], F32, kind="ExternalInput").ap()
    wq = dt("wq", [H, H], F32, kind="ExternalInput").ap()
    wk = dt("wk", [H, H], F32, kind="ExternalInput").ap()
    wv = dt("wv", [H, H], F32, kind="ExternalInput").ap()
    wo = dt("wo", [H, H], F32, kind="ExternalInput").ap()
    bq = dt("bq", [128, H // 128], F32, kind="ExternalInput").ap()
    bv = dt("bv", [H, 1], F32, kind="ExternalInput").ap()
    bo = dt("bo", [128, H // 128], F32, kind="ExternalInput").ap()
    gate_w = dt("gate_w", [H, E], F32, kind="ExternalInput").ap()
    sh_gate = dt("sh_gate", [H, I], F32, kind="ExternalInput").ap()
    sh_up = dt("sh_up", [H, I], F32, kind="ExternalInput").ap()
    sh_down = dt("sh_down", [I, H], F32, kind="ExternalInput").ap()
    rt_gate = dt("rt_gate", [E, H, I], F32, kind="ExternalInput").ap()
    rt_up = dt("rt_up", [E, H, I], F32, kind="ExternalInput").ap()
    rt_down = dt("rt_down", [E, I, H], F32, kind="ExternalInput").ap()

    hostc = build_host_constants(cfg)
    cst = {k: dt(k, list(v.shape), F32, kind="ExternalInput").ap()
           for k, v in hostc.items()}

    X = dt("Xres", [H, NTOK], F32).ap()
    QKVa = dt("QKVa", [3 * H, NTOK], BF16).ap()
    AO = dt("AOscr", [H, NTOK], BF16).ap()
    Qb = dt("Qbscr", [H, NTOK], BF16).ap()
    Kb = dt("Kbscr", [H, NTOK], BF16).ap()
    NG = NTOK // 128
    Vb = dt("Vbscr", [NH, NG, 128, HS], BF16).ap()
    NQC = T // 128
    STROW = dt("strow", [2, NQC * 256], F32).ap()
    STROW2 = dt("strow2", [4, 2 * T], BF16).ap()

    ctx = Ctx(nc)
    pe, dve, act, pool, sp = ctx.pe, ctx.dve, ctx.act, ctx.pool, ctx.sp
    esg = ExitStack()

    def sbuf(name, shape, dtyp, stk=esg):
        return stk.enter_context(nc.sbuf_tensor(name, shape, dtyp))

    def psum(name, shape, dtyp, stk=esg):
        return stk.enter_context(nc.psum_tensor(name, shape, dtyp))

    g_id = sbuf("g_id", [128, 128], BF16)
    g_idf = sbuf("g_idf", [128, 128], F32)
    g_oc = sbuf("g_oc", [128, 1], BF16)
    g_ocf = sbuf("g_ocf", [128, 1], F32)
    g_or = sbuf("g_or", [1, 128], BF16)
    g_orf = sbuf("g_orf", [1, 128], F32)
    g_esel = sbuf("g_esel", [8, 8 * 128], BF16)
    g_e64 = sbuf("g_e64", [1, 128], BF16)
    g_gw = sbuf("g_gw", [128, HC * E], BF16)
    g_bq = sbuf("g_bq", [128, HC], F32)
    g_bv = sbuf("g_bvr", [1, H], BF16)
    g_bo = sbuf("g_bo", [128, HC], F32)
    g_cos = sbuf("g_cos", [128, cfg.ROPE_P], BF16)
    g_sin = sbuf("g_sin", [128, cfg.ROPE_P], BF16)
    bC = ctx.buf("gconst")

    ctx.new_phase_sems()
    Qcp = ctx.queue("const_p")
    Qcs = ctx.queue("const_s")
    pool.dma(lambda e: e.dma_start(g_id[:, :], cst["c_idn"]), Qcp, outs=(bC,))
    pool.dma(lambda e: e.dma_start(g_oc[:, :], cst["c_ones_col"]), Qcp, outs=(bC,))
    pool.dma(lambda e: e.dma_start(g_or[:, :], cst["c_ones_row"]), Qcp, outs=(bC,))
    pool.dma(lambda e: e.dma_start(g_esel[:, :], cst["c_esel"]), Qcp, outs=(bC,))
    pool.dma(lambda e: e.dma_start(g_e64[:, :], cst["c_e64"]), Qcp, outs=(bC,))
    pool.dma(lambda e: e.dma_start(
        g_gw[:, :], gate_w.rearrange("(c p) n -> p c n", p=128)), Qcp, outs=(bC,))
    pool.dma(lambda e: e.dma_start(g_cos[:, :], cos2), Qcp, outs=(bC,))
    pool.dma(lambda e: e.dma_start(g_sin[:, :], sin2), Qcp, outs=(bC,))
    pool.dma(lambda e: e.dma_start(
        g_bv[:, :], bv.rearrange("a b -> b a")), Qcp, outs=(bC,))
    sp.dma(lambda e: e.dma_start(g_idf[:, :], cst["c_idn"]), Qcs, outs=(bC,))
    sp.dma(lambda e: e.dma_start(g_ocf[:, :], cst["c_ones_col"]), Qcs, outs=(bC,))
    sp.dma(lambda e: e.dma_start(g_orf[:, :], cst["c_ones_row"]), Qcs, outs=(bC,))
    sp.dma(lambda e: e.dma_start(g_bq[:, :], bq), Qcs, outs=(bC,))
    sp.dma(lambda e: e.dma_start(g_bo[:, :], bo), Qcs, outs=(bC,))
    bC.w = (Qcp, Qcp.count)
    ctx.sp.wait((Qcs, Qcs.count))
    bC2 = ctx.buf("gconst2")
    bC2.w = (Qcs, Qcs.count)

    class LNKit:
        def __init__(self, tag, TT, stk, pstk, ps_a=None, ps_b=None):
            self.TT = TT
            self.qx = ctx.queue(tag + "_qx")
            self.xt = sbuf(f"{tag}_xt", [128, HC * TT], F32, stk)
            self.xb = sbuf(f"{tag}_xb", [128, HC * TT], BF16, stk)
            self.sq = sbuf(f"{tag}_sq", [128, HC * TT], BF16, stk)
            self.rm = sbuf(f"{tag}_rm", [1, TT], F32, stk)
            self.rv = sbuf(f"{tag}_rv", [1, TT], F32, stk)
            self.rr = sbuf(f"{tag}_rr", [1, TT], F32, stk)
            self.rb = sbuf(f"{tag}_rb", [1, TT], F32, stk)
            self.pa = ps_a if ps_a is not None else psum(f"{tag}_pa", [128, 512], F32, pstk)
            self.pb = ps_b if ps_b is not None else psum(f"{tag}_pb", [128, 512], F32, pstk)
            self.pm = self.pa
            self.ps = self.pb
            self.b_xt = ctx.buf(f"{tag}_xt")
            self.b_xb = ctx.buf(f"{tag}_xb")
            self.b_sq = ctx.buf(f"{tag}_sq")
            self.b_pa = ctx.buf(f"{tag}_pa")
            self.b_pb = ctx.buf(f"{tag}_pb")
            self.b_pm = self.b_pa
            self.b_ps = self.b_pb
            self.b_rm = ctx.buf(f"{tag}_rm")
            self.b_rv = ctx.buf(f"{tag}_rv")
            self.b_rr = ctx.buf(f"{tag}_rr")
            self.b_rb = ctx.buf(f"{tag}_rb")

    def emit_ln(kit, src_dram, col0, ln_sb, b_ln, TT=None):
        TT = TT or kit.TT
        assert TT == kit.TT
        xt, xb, sq = kit.xt, kit.xb, kit.sq
        sp.dma(lambda e: e.dma_start(
            xt[:, :],
            src_dram[:, col0:col0 + TT].rearrange("(c p) n -> p c n", p=128)),
            kit.qx, outs=(kit.b_xt,))
        act.op(lambda e: e.copy(xb[:, 0:HC * TT], xt[:, 0:HC * TT]),
               ins=(kit.b_xt,), outs=(kit.b_xb,))
        act.op(lambda e: e.activation(sq[:, 0:HC * TT], xb[:, 0:HC * TT],
                                      AF.Square),
               ins=(kit.b_xb,), outs=(kit.b_sq,))
        rH = 1.0 / H
        ncg = cdiv(TT, 512)
        for cg in range(ncg):
            n0 = cg * 512
            n1 = min(TT, n0 + 512)
            w = n1 - n0
            def fm(e, n0=n0, w=w):
                for c in range(HC):
                    i = e.matmul(kit.pm[0:1, 0:w], g_oc[:, :],
                                 xb[:, c * TT + n0:c * TT + n0 + w],
                                 start=(c == 0), stop=(c == HC - 1))
                return i
            def fs(e, n0=n0, w=w):
                for c in range(HC):
                    i = e.matmul(kit.ps[0:1, 0:w], g_oc[:, :],
                                 sq[:, c * TT + n0:c * TT + n0 + w],
                                 start=(c == 0), stop=(c == HC - 1))
                return i
            pe.op(fm, ins=(kit.b_xb, bC), outs=(kit.b_pm,))
            pe.op(fs, ins=(kit.b_sq, bC), outs=(kit.b_ps,))
            dve.op(lambda e, n0=n0, w=w: e.tensor_scalar(
                kit.rm[0:1, n0:n0 + w], kit.pm[0:1, 0:w], rH, None, op0=ALU.mult),
                ins=(kit.b_pm,), outs=(kit.b_rm,))
            dve.op(lambda e, n0=n0, w=w: e.tensor_scalar(
                kit.rv[0:1, n0:n0 + w], kit.ps[0:1, 0:w], rH, None, op0=ALU.mult),
                ins=(kit.b_ps,), outs=(kit.b_rv,))
        dve.op(lambda e: e.tensor_mul(kit.rb[0:1, 0:TT], kit.rm[0:1, 0:TT],
                                      kit.rm[0:1, 0:TT]),
               ins=(kit.b_rm,), outs=(kit.b_rb,))
        dve.op(lambda e: e.tensor_sub(kit.rv[0:1, 0:TT], kit.rv[0:1, 0:TT],
                                      kit.rb[0:1, 0:TT]),
               ins=(kit.b_rv, kit.b_rb), outs=(kit.b_rv,))
        dve.op(lambda e: e.tensor_scalar(kit.rv[0:1, 0:TT], kit.rv[0:1, 0:TT],
                                         1e-5, None, op0=ALU.add),
               ins=(kit.b_rv,), outs=(kit.b_rv,))
        act.op(lambda e: e.activation(kit.rr[0:1, 0:TT], kit.rv[0:1, 0:TT],
                                      AF.Ln),
               ins=(kit.b_rv,), outs=(kit.b_rr,))
        act.op(lambda e: e.activation(kit.rr[0:1, 0:TT], kit.rr[0:1, 0:TT],
                                      AF.Exp, scale=-0.5),
               ins=(kit.b_rr,), outs=(kit.b_rr,))
        dve.op(lambda e: e.tensor_mul(kit.rb[0:1, 0:TT], kit.rm[0:1, 0:TT],
                                      kit.rr[0:1, 0:TT]),
               ins=(kit.b_rm, kit.b_rr), outs=(kit.b_rb,))
        for cg in range(ncg):
            n0 = cg * 512
            n1 = min(TT, n0 + 512)
            w = n1 - n0
            pe.op(lambda e, n0=n0, w=w: e.matmul(
                kit.pa[:, 0:w], g_orf[:, :], kit.rr[0:1, n0:n0 + w],
                start=True, stop=True),
                ins=(kit.b_rr, bC), outs=(kit.b_pa,))
            pe.op(lambda e, n0=n0, w=w: e.matmul(
                kit.pb[:, 0:w], g_orf[:, :], kit.rb[0:1, n0:n0 + w],
                start=True, stop=True),
                ins=(kit.b_rb, bC), outs=(kit.b_pb,))
            for c in range(HC):
                dve.op(lambda e, c=c, n0=n0, w=w: e.tensor_mul(
                    sq[:, c * TT + n0:c * TT + n0 + w],
                    xt[:, c * TT + n0:c * TT + n0 + w], kit.pa[:, 0:w]),
                    ins=(kit.b_xt, kit.b_pa), outs=(kit.b_sq,))
                dve.op(lambda e, c=c, n0=n0, w=w: e.tensor_sub(
                    ln_sb[:, c * TT + n0:c * TT + n0 + w],
                    sq[:, c * TT + n0:c * TT + n0 + w], kit.pb[:, 0:w]),
                    ins=(kit.b_sq, kit.b_pb), outs=(b_ln,))

    def phase_p1():
        ctx.new_phase_sems()
        stk, pstk = ExitStack(), ExitStack()
        JC = 3 * HC
        wsb = sbuf("p1_w", [128, HC * 3 * H], BF16, stk)
        bW = ctx.buf("p1_w")
        pool.dma(lambda e: e.dma_start(
            wsb[:, :], w_attnT.rearrange("(c p) n -> p c n", p=128)),
            ctx.queue("p1_w"), outs=(bW,))
        kit = LNKit("p1ln", GS, stk, pstk)
        Qst = [ctx.queue("p1_st0"), ctx.queue("p1_st1")]
        ln = [sbuf(f"p1_ln{i}", [128, HC * GS], BF16, stk) for i in range(2)]
        bL = [ctx.buf(f"p1_ln{i}") for i in range(2)]
        pq = [psum(f"p1_pq{i}", [128, GS], F32, pstk) for i in range(2)]
        bQ = [ctx.buf(f"p1_pq{i}") for i in range(2)]
        ev = [sbuf(f"p1_ev{i}", [128, GS], BF16, stk) for i in range(2)]
        bE = [ctx.buf(f"p1_ev{i}") for i in range(2)]
        for ti in range(NTOK // GS):
            par = ti % 2
            emit_ln(kit, x_in, ti * GS, ln[par], bL[par])
            for jc in range(JC):
                pp = jc % 2
                def fm(e, jc=jc, par=par, pp=pp):
                    for c in range(HC):
                        i = e.matmul(pq[pp][:, :],
                                     wsb[:, c * 3 * H + jc * 128:
                                         c * 3 * H + (jc + 1) * 128],
                                     ln[par][:, c * GS:(c + 1) * GS],
                                     start=(c == 0), stop=(c == HC - 1))
                    return i
                pe.op(fm, ins=(bW, bL[par]), outs=(bQ[pp],))
                if HC <= jc < 2 * HC:
                    act.op(lambda e, pp=pp: e.activation(
                        ev[pp][:, :], pq[pp][:, :], AF.Copy, scale=scal),
                        ins=(bQ[pp],), outs=(bE[pp],))
                else:
                    act.op(lambda e, pp=pp: e.copy(ev[pp][:, :], pq[pp][:, :]),
                           ins=(bQ[pp],), outs=(bE[pp],))
                sp.dma(lambda e, jc=jc, ti=ti, pp=pp: e.dma_start(
                    QKVa[jc * 128:(jc + 1) * 128, ti * GS:(ti + 1) * GS],
                    ev[pp][:, :]), Qst[pp], ins=(bE[pp],))
        ctx.barrier()
        stk.close()
        pstk.close()

    def phase_p2():
        ctx.new_phase_sems()
        stk, pstk = ExitStack(), ExitStack()
        NSC = SP1 // 128
        KCW = min(512, SP1)
        NKC = SP1 // KCW
        NQB = cdiv(T, 512)
        QB = min(512, T)
        maskP1 = sbuf("p2_m1", [128, NQC * SP1], BF16, stk)
        maskP2 = sbuf("p2_m2", [128, NSC * T], BF16, stk)
        m4a = sbuf("p2_m4a", [64, T], BF16, stk)
        m4b = sbuf("p2_m4b", [64, T], BF16, stk)
        bM = ctx.buf("p2_cst")
        Qm = ctx.queue("p2_mask")
        pool.dma(lambda e: e.dma_start(
            maskP1[:, :], cst["c_mask_add"].rearrange("(c p) n -> p c n", p=128)),
            Qm, outs=(bM,))
        pool.dma(lambda e: e.dma_start(
            maskP2[:, :], cst["c_maskT_add"].rearrange("(c p) n -> p c n", p=128)),
            Qm, outs=(bM,))
        pool.dma(lambda e: e.dma_start(m4a[:, :], cst["c_m4a"]), Qm, outs=(bM,))
        pool.dma(lambda e: e.dma_start(m4b[:, :], cst["c_m4b"]), Qm, outs=(bM,))
        bM.w = (Qm, Qm.count)
        Qlq = [ctx.queue("p2_lq0"), ctx.queue("p2_lq1")]
        Qlk = [ctx.queue("p2_lk0"), ctx.queue("p2_lk1")]
        Qlv = [ctx.queue("p2_lv0"), ctx.queue("p2_lv1")]
        QvT = [ctx.queue("p2_vT0"), ctx.queue("p2_vT1")]
        Qsr = [ctx.queue("p2_sr0"), ctx.queue("p2_sr1")]
        Qsrb = [ctx.queue("p2_srb0"), ctx.queue("p2_srb1")]
        Qao = [ctx.queue("p2_ao0"), ctx.queue("p2_ao1")]

        def mk2(nm, shape, dtyp):
            return ([sbuf(f"{nm}{i}", shape, dtyp, stk) for i in range(2)],
                    [ctx.buf(f"{nm}{i}") for i in range(2)])
        q2, bq2 = mk2("p2_q2", [66, T], BF16)
        kt, bkt = mk2("p2_kt", [64, T], BF16)
        vt, bvt = mk2("p2_vt", [64, T], BF16)
        ks, bks = mk2("p2_ks", [66, SP1], BF16)
        vs, bvs = mk2("p2_vs", [64, SP1], BF16)
        vT, bvT = mk2("p2_vT", [128, NSC * 64], BF16)
        ktl, bktl = mk2("p2_ktl", [64, T], BF16)
        vtlb, bvtlb = mk2("p2_vtlb", [64, T], BF16)
        pbt, bpbt = mk2("p2_pbt", [64, T], BF16)
        tre1 = sbuf("p2_tre", [64, T], BF16, stk)
        btre1 = ctx.buf("p2_tre")
        tre, btre = [tre1, tre1], [btre1, btre1]
        trs1 = sbuf("p2_trs", [64, SP1], F32, stk)
        btrs1 = ctx.buf("p2_trs")
        trs, btrs = [trs1, trs1], [btrs1, btrs1]
        tailr, btailr = mk2("p2_tailr", [1, T], BF16)
        strow_sb, bstrow = mk2("p2_str", [NQC, 256], F32)
        statr, bstatr = mk2("p2_statr", [1, T], F32)
        stlo, bstlo = mk2("p2_stlo", [1, T], F32)
        sfh, bsfh = mk2("p2_sfh", [128, NQC], BF16)
        sfl, bsfl = mk2("p2_sfl", [128, NQC], BF16)
        attP, battP = mk2("p2_attP", [1, T], BF16)
        ncv, bncv = mk2("p2_ncv", [128, NQC * NKC], F32)
        scv, bscv = mk2("p2_scv", [128, NQC * NKC], F32)
        stf, bstf = mk2("p2_stf", [128, NQC], F32)
        junk = sbuf("p2_junk", [128, KCW], BF16, stk)
        bjunk = ctx.buf("p2_junk")
        attb = [sbuf(f"p2_attb{i}", [128, QB], BF16, stk) for i in range(4)]
        battb = [ctx.buf(f"p2_attb{i}") for i in range(4)]
        osb, bosb = mk2("p2_osb", [64, QB], BF16)
        tcon, btcon = mk2("p2_tcon", [64, QB], BF16)
        sgl, bsgl = mk2("p2_sgl", [128, 3], F32)

        ps1 = [psum(f"p2_ps1{i}", [128, KCW], F32, pstk) for i in range(2)]
        bps1 = [ctx.buf(f"p2_ps1{i}") for i in range(2)]
        ps2 = [psum(f"p2_ps2{i}", [128, 512], F32, pstk) for i in range(4)]
        bps2 = [ctx.buf(f"p2_ps2{i}") for i in range(4)]
        pso2 = [psum(f"p2_pso{i}", [64, QB], F32, pstk) for i in range(2)]
        bpso2 = [ctx.buf(f"p2_pso{i}") for i in range(2)]

        rset = [b for (_, b) in cfg.sets]

        def gen(it_idx, s, h):
            p = it_idx % 2
            slot = it_idx % 2
            c0 = s * T
            psA, bpsA = ps2[slot * 2], bps2[slot * 2]
            pso, bpso = pso2[slot], bpso2[slot]
            sp.dma(lambda e, h=h, c0=c0, p=p: e.dma_start(
                q2[p][0:64, :], QKVa[h * HS:(h + 1) * HS, c0:c0 + T]),
                Qlq[p], outs=(bq2[p],))
            sp.dma(lambda e, h=h, c0=c0, p=p: e.dma_start(
                kt[p][:, :], QKVa[H + h * HS:H + (h + 1) * HS, c0:c0 + T]),
                Qlk[p], outs=(bkt[p],))
            sp.dma(lambda e, h=h, c0=c0, p=p: e.dma_start(
                vt[p][:, :], QKVa[2 * H + h * HS:2 * H + (h + 1) * HS,
                                  c0:c0 + T]), Qlv[p], outs=(bvt[p],))
            yield
            def emit_tree(srcb, bsrc, dstb, bdst, p=p):
                pool.op(lambda e: e.tensor_add(
                    tre[p][:, 0:T // 2], srcb[:, 0:T:2], srcb[:, 1:T:2]),
                    ins=(bsrc,), outs=(btre[p],))
                n = T // 4
                pool.op(lambda e, n=n: e.tensor_add(
                    trs[p][:, 0:n], tre[p][:, 0:T // 2:2],
                    tre[p][:, 1:T // 2:2]),
                    ins=(btre[p],), outs=(btrs[p],))
                po, pn = 0, n
                L = 8
                while L <= T:
                    n = T // L
                    off = po + pn
                    pool.op(lambda e, n=n, off=off, po=po, pn=pn:
                            e.tensor_add(trs[p][:, off:off + n],
                                         trs[p][:, po:po + pn:2],
                                         trs[p][:, po + 1:po + pn:2]),
                            ins=(btrs[p],), outs=(btrs[p],))
                    po, pn = off, n
                    L *= 2
                act.op(lambda e: e.copy(dstb[0:64, 0:SP1 - 1],
                                        trs[p][:, 0:SP1 - 1]),
                       ins=(btrs[p],), outs=(bdst,))
            emit_tree(kt[p], bkt[p], ks[p], bks[p])
            yield
            emit_tree(vt[p], bvt[p], vs[p], bvs[p])
            def fz(e, p=p):
                e.memset(ks[p][0:64, SP1 - 1:SP1], 0.0)
                e.memset(vs[p][:, SP1 - 1:SP1], 0.0)
                return e.memset(ks[p][64:66, :], 1.0)
            pool.op(fz, outs=(bks[p], bvs[p]))
            yield
            def emit_tails(srcb, bsrc, dstb, bdst, p=p):
                def t1(e):
                    e.tensor_mul(tre[p][:, 1:T], srcb[:, 0:T - 1],
                                 m4a[:, 1:T])
                    return e.memset(tre[p][:, 0:1], 0.0)
                pool.op(t1, ins=(bsrc, bM), outs=(btre[p],))
                pool.op(lambda e: e.tensor_add(tre[p][:, :], tre[p][:, :],
                                               srcb[:, :]),
                        ins=(btre[p], bsrc), outs=(btre[p],))
                def t2(e):
                    e.tensor_mul(dstb[:, 2:T], tre[p][:, 0:T - 2],
                                 m4b[:, 2:T])
                    return e.memset(dstb[:, 0:2], 0.0)
                pool.op(t2, ins=(btre[p], bM), outs=(bdst,))
                pool.op(lambda e: e.tensor_add(dstb[:, :], dstb[:, :],
                                               tre[p][:, :]),
                        ins=(bdst, btre[p]), outs=(bdst,))
            emit_tails(kt[p], bkt[p], ktl[p], bktl[p])
            yield
            emit_tails(vt[p], bvt[p], vtlb[p], bvtlb[p])
            yield
            for sc_i in range(NSC):
                def ftr(e, sc_i=sc_i, p=p):
                    return e.dma_start(
                        vT[p][:, sc_i * 64:(sc_i + 1) * 64],
                        vs[p][:, sc_i * 128:(sc_i + 1) * 128],
                        transpose=True)
                sp.dma(ftr, QvT[p], ins=(bvs[p],), outs=(bvT[p],))
            bvT[p].w = (QvT[p], QvT[p].count)
            pool.op(lambda e, p=p: e.tensor_mul(pbt[p][:, :], q2[p][0:64, :],
                                                ktl[p][:, :]),
                    ins=(bq2[p], bktl[p]), outs=(bpbt[p],))
            yield
            for cg in range(cdiv(T, 512)):
                n0 = cg * 512
                w = min(T, n0 + 512) - n0
                pe.op(lambda e, n0=n0, w=w, p=p, psA=psA: e.matmul(
                    psA[0:1, 0:w], g_oc[0:64, :], pbt[p][:, n0:n0 + w],
                    start=True, stop=True),
                    ins=(bpbt[p], bC), outs=(bpsA,))
                dve.op(lambda e, n0=n0, w=w, p=p, psA=psA: e.tensor_copy(
                    tailr[p][0:1, n0:n0 + w], psA[0:1, 0:w]),
                    ins=(bpsA,), outs=(btailr[p],))
            yield
            for qc in range(NQC):
                for kc in range(NKC):
                    pp = kc % 2
                    n0 = kc * KCW
                    def f1(e, qc=qc, kc=kc, n0=n0, p=p, pp=pp):
                        wset = KCW - 1 if kc == NKC - 1 else KCW
                        i = e.matmul(ps1[pp][:, 0:wset],
                                     q2[p][0:64, qc * 128:(qc + 1) * 128],
                                     ks[p][0:64, n0:n0 + wset],
                                     start=True, stop=True)
                        if kc == NKC - 1:
                            i = e.matmul(
                                ps1[pp][:, KCW - 1:KCW],
                                tailr[p][0:1, qc * 128:(qc + 1) * 128],
                                g_or[0:1, 0:1], start=True, stop=True)
                        return i
                    pe.op(f1, ins=(bq2[p], bks[p], btailr[p], bC),
                          outs=(bps1[pp],))
                    jj = qc * NKC + kc
                    dve.op(lambda e, qc=qc, kc=kc, pp=pp: e.tensor_add(
                        ps1[pp][:, :], ps1[pp][:, :],
                        maskP1[:, qc * SP1 + kc * KCW:
                               qc * SP1 + (kc + 1) * KCW]),
                        ins=(bM, bps1[pp]), outs=(bps1[pp],))
                    dve.op(lambda e, jj=jj, pp=pp, p=p: e.reduce_max(
                        ncv[p][:, jj:jj + 1], ps1[pp][:, :], axis=AX.X,
                        negate=True),
                        ins=(bps1[pp],), outs=(bncv[p],))
                    def f2(e, jj=jj, pp=pp, p=p):
                        return e.activation(
                            junk[:, :], ps1[pp][:, :], AF.Exp,
                            bias=ncv[p][:, jj:jj + 1],
                            accum_out=scv[p][:, jj:jj + 1])
                    act.op(f2, ins=(bps1[pp], bncv[p]),
                           outs=(bjunk, bscv[p]))
                jj0 = qc * NKC
                if NKC == 1:
                    act.op(lambda e, jj0=jj0, p=p: e.activation(
                        sgl[p][:, 2:3], scv[p][:, jj0:jj0 + 1], AF.Ln),
                        ins=(bscv[p],), outs=(bsgl[p],))
                    dve.op(lambda e, qc=qc, jj0=jj0, p=p: e.tensor_sub(
                        stf[p][:, qc:qc + 1], ncv[p][:, jj0:jj0 + 1],
                        sgl[p][:, 2:3]),
                        ins=(bncv[p], bsgl[p]), outs=(bstf[p],))
                else:
                    dve.op(lambda e, jj0=jj0, p=p: e.tensor_reduce(
                        sgl[p][:, 0:1], ncv[p][:, jj0:jj0 + NKC], axis=AX.X,
                        op=ALU.min),
                        ins=(bncv[p],), outs=(bsgl[p],))
                    dve.op(lambda e, jj0=jj0, p=p: e.tensor_scalar(
                        ncv[p][:, jj0:jj0 + NKC], ncv[p][:, jj0:jj0 + NKC],
                        sgl[p][:, 0:1], None, op0=ALU.subtract),
                        ins=(bncv[p], bsgl[p]), outs=(bncv[p],))
                    act.op(lambda e, jj0=jj0, p=p: e.activation(
                        ncv[p][:, jj0:jj0 + NKC], ncv[p][:, jj0:jj0 + NKC],
                        AF.Exp, scale=-1.0),
                        ins=(bncv[p],), outs=(bncv[p],))
                    dve.op(lambda e, jj0=jj0, p=p: e.tensor_mul(
                        scv[p][:, jj0:jj0 + NKC], scv[p][:, jj0:jj0 + NKC],
                        ncv[p][:, jj0:jj0 + NKC]),
                        ins=(bscv[p], bncv[p]), outs=(bscv[p],))
                    dve.op(lambda e, jj0=jj0, p=p: e.reduce_sum(
                        sgl[p][:, 1:2], scv[p][:, jj0:jj0 + NKC], axis=AX.X),
                        ins=(bscv[p],), outs=(bsgl[p],))
                    act.op(lambda e, p=p: e.activation(sgl[p][:, 2:3],
                                                       sgl[p][:, 1:2], AF.Ln),
                           ins=(bsgl[p],), outs=(bsgl[p],))
                    dve.op(lambda e, qc=qc, p=p: e.tensor_sub(
                        stf[p][:, qc:qc + 1], sgl[p][:, 0:1], sgl[p][:, 2:3]),
                        ins=(bsgl[p],), outs=(bstf[p],))
                yield
            act.op(lambda e, p=p: e.copy(sfh[p][:, :], stf[p][:, :]),
                   ins=(bstf[p],), outs=(bsfh[p],))
            dve.op(lambda e, p=p: e.tensor_sub(
                sfl[p][:, :], stf[p][:, :], sfh[p][:, :]),
                ins=(bstf[p], bsfh[p]), outs=(bsfl[p],))
            pe.op(lambda e, p=p, psA=psA: e.matmul(
                psA[0:NQC, 0:128], sfh[p][:, :], g_id[:, :],
                start=True, stop=True),
                ins=(bsfh[p], bC), outs=(bpsA,))
            dve.op(lambda e, p=p, psA=psA: e.tensor_copy(
                strow_sb[p][:, 0:128], psA[0:NQC, 0:128]),
                ins=(bpsA,), outs=(bstrow[p],))
            pe.op(lambda e, p=p, psA=psA: e.matmul(
                psA[0:NQC, 0:128], sfl[p][:, :], g_id[:, :],
                start=True, stop=True),
                ins=(bsfl[p], bC), outs=(bpsA,))
            dve.op(lambda e, p=p, psA=psA: e.tensor_copy(
                strow_sb[p][:, 128:256], psA[0:NQC, 0:128]),
                ins=(bpsA,), outs=(bstrow[p],))
            sp.dma(lambda e, p=p: e.dma_start(
                STROW[p:p + 1, :].rearrange("o (a b) -> (o a) b", b=256),
                strow_sb[p][:, :]), Qsr[p],
                ins=(bstrow[p],), outs=(bstrow[p],))
            sp.dma(lambda e, p=p: e.dma_start(
                statr[p][:, :],
                STROW[p:p + 1, :].rearrange("o (a b) -> (o a) b", b=256)[:, 0:128]),
                Qsr[p], ins=(bstrow[p],), outs=(bstatr[p],))
            sp.dma(lambda e, p=p: e.dma_start(
                stlo[p][:, :],
                STROW[p:p + 1, :].rearrange("o (a b) -> (o a) b", b=256)[:, 128:256]),
                Qsr[p], ins=(bstrow[p],), outs=(bstlo[p],))
            pool.dma(lambda e, p=p: e.dma_start(
                q2[p][64:65, :],
                STROW[p:p + 1, :].rearrange("o (a b) -> (o a) b", b=256)[:, 0:128]),
                Qsrb[p], ins=(bstrow[p],), outs=(bq2[p],))
            pool.dma(lambda e, p=p: e.dma_start(
                q2[p][65:66, :],
                STROW[p:p + 1, :].rearrange("o (a b) -> (o a) b", b=256)[:, 128:256]),
                Qsrb[p], ins=(bstrow[p],), outs=(bq2[p],))
            dve.op(lambda e, p=p: e.tensor_add(
                statr[p][0:1, :], statr[p][0:1, :], stlo[p][0:1, :]),
                ins=(bstatr[p], bstlo[p]), outs=(bstatr[p],))
            dve.op(lambda e, p=p: e.tensor_add(
                statr[p][0:1, :], statr[p][0:1, :], tailr[p][0:1, :]),
                ins=(bstatr[p], btailr[p]), outs=(bstatr[p],))
            act.op(lambda e, p=p: e.activation(
                attP[p][0:1, :], statr[p][0:1, :], AF.Exp),
                ins=(bstatr[p],), outs=(battP[p],))
            yield
            for qb in range(NQB):
                qn0 = qb * QB
                qmax = qn0 + QB - 1
                incl = []
                needm = []
                for sc_i in range(NSC):
                    lo = sc_i * 128
                    hi = min(SP1 - 1, (sc_i + 1) * 128)
                    rs = rset[lo:hi]
                    if not rs or min(rs) > qmax:
                        continue
                    incl.append(sc_i)
                    needm.append(max(rs) > qn0 or sc_i == NSC - 1)
                for ii, sc_i in enumerate(incl):
                    pb2 = slot * 2 + 1 if ii % 2 else slot * 2
                    pb2 = slot * 2 + (ii % 2)
                    def f5(e, sc_i=sc_i, qn0=qn0, p=p, pb2=pb2):
                        return e.matmul(
                            ps2[pb2][:, 0:QB],
                            ks[p][:, sc_i * 128:(sc_i + 1) * 128],
                            q2[p][:, qn0:qn0 + QB],
                            start=True, stop=True)
                    pe.op(f5, ins=(bks[p], bq2[p]), outs=(bps2[pb2],))
                    if needm[ii]:
                        dve.op(lambda e, sc_i=sc_i, qn0=qn0, pb2=pb2:
                               e.tensor_add(
                                   ps2[pb2][:, 0:QB], ps2[pb2][:, 0:QB],
                                   maskP2[:, sc_i * T + qn0:
                                          sc_i * T + qn0 + QB]),
                               ins=(bM, bps2[pb2]), outs=(bps2[pb2],))
                    ab = slot * 2 + (ii % 2)
                    act.op(lambda e, pb2=pb2, ab=ab: e.activation(
                        attb[ab][:, :], ps2[pb2][:, 0:QB], AF.Exp),
                        ins=(bps2[pb2],), outs=(battb[ab],))
                    yield
                    def f6(e, sc_i=sc_i, p=p, ab=ab, pso=pso,
                           first=(ii == 0), last=(ii == len(incl) - 1)):
                        return e.matmul(
                            pso[:, :], vT[p][:, sc_i * 64:(sc_i + 1) * 64],
                            attb[ab][:, :], start=first, stop=last)
                    pe.op(f6, ins=(bvT[p], battb[ab]), outs=(bpso,))
                pe.op(lambda e, qn0=qn0, p=p, psA=psA: e.matmul(
                    psA[0:64, 0:QB], g_or[0:1, 0:64],
                    attP[p][0:1, qn0:qn0 + QB], start=True, stop=True),
                    ins=(battP[p], bC), outs=(bpsA,))
                dve.op(lambda e, qn0=qn0, p=p, psA=psA: e.tensor_mul(
                    tcon[p][:, :], vtlb[p][:, qn0:qn0 + QB],
                    psA[0:64, 0:QB]),
                    ins=(bvtlb[p], bpsA), outs=(btcon[p],))
                dve.op(lambda e, p=p, pso=pso: e.tensor_add(
                    osb[p][:, :], pso[:, :], tcon[p][:, :]),
                    ins=(bpso, btcon[p]), outs=(bosb[p],))
                sp.dma(lambda e, h=h, c0=c0, qn0=qn0, p=p: e.dma_start(
                    AO[h * HS:(h + 1) * HS, c0 + qn0:c0 + qn0 + QB],
                    osb[p][:, :]), Qao[p], ins=(bosb[p],))
                yield

        gens = []
        it_idx = 0
        for s in range(SEQ):
            for h in range(NH):
                gens.append(gen(it_idx, s, h))
                it_idx += 1
        drive(gens, W=2)
        ctx.barrier()
        stk.close()
        pstk.close()

    def phase_proj(tag, wT_dram, res_src, dst, bias_sb=None):
        ctx.new_phase_sems()
        stk, pstk = ExitStack(), ExitStack()
        wsb = sbuf(f"{tag}_w", [128, HC * H], BF16, stk)
        bW = ctx.buf(f"{tag}_w")
        pool.dma(lambda e: e.dma_start(
            wsb[:, :], wT_dram.rearrange("(c p) n -> p c n", p=128)),
            ctx.queue(tag + "_w"), outs=(bW,))
        Qa = [ctx.queue(tag + "_a0"), ctx.queue(tag + "_a1")]
        Qx = [ctx.queue(tag + "_x0"), ctx.queue(tag + "_x1")]
        Qo = [ctx.queue(tag + "_o0"), ctx.queue(tag + "_o1")]
        ao = [sbuf(f"{tag}_ao{i}", [128, HC * GS], BF16, stk) for i in range(2)]
        bA = [ctx.buf(f"{tag}_ao{i}") for i in range(2)]
        xr = [sbuf(f"{tag}_xr{i}", [128, HC * GS], F32, stk) for i in range(2)]
        bX = [ctx.buf(f"{tag}_xr{i}") for i in range(2)]
        xo = [sbuf(f"{tag}_xo{i}", [128, HC * GS], F32, stk) for i in range(2)]
        bO = [ctx.buf(f"{tag}_xo{i}") for i in range(2)]
        pp_ = [psum(f"{tag}_pp{i}", [128, GS], F32, pstk) for i in range(2)]
        bP = [ctx.buf(f"{tag}_pp{i}") for i in range(2)]
        for ti in range(NTOK // GS):
            par = ti % 2
            sp.dma(lambda e, ti=ti, par=par: e.dma_start(
                ao[par][:, :],
                AO[:, ti * GS:(ti + 1) * GS].rearrange("(c p) n -> p c n", p=128)),
                Qa[par], outs=(bA[par],))
            sp.dma(lambda e, ti=ti, par=par: e.dma_start(
                xr[par][:, :],
                res_src[:, ti * GS:(ti + 1) * GS].rearrange("(c p) n -> p c n",
                                                            p=128)),
                Qx[par], outs=(bX[par],))
            for jc in range(HC):
                pp2 = jc % 2
                def fm(e, jc=jc, par=par, pp2=pp2):
                    for c in range(HC):
                        i = e.matmul(pp_[pp2][:, :],
                                     wsb[:, c * H + jc * 128:c * H + (jc + 1) * 128],
                                     ao[par][:, c * GS:(c + 1) * GS],
                                     start=(c == 0), stop=(c == HC - 1))
                    return i
                pe.op(fm, ins=(bW, bA[par]), outs=(bP[pp2],))
                if bias_sb is not None:
                    def fe(e, jc=jc, par=par, pp2=pp2):
                        return e.scalar_tensor_tensor(
                            xo[par][:, jc * GS:(jc + 1) * GS], pp_[pp2][:, :],
                            bias_sb[:, jc:jc + 1],
                            xr[par][:, jc * GS:(jc + 1) * GS],
                            op0=ALU.add, op1=ALU.add)
                else:
                    def fe(e, jc=jc, par=par, pp2=pp2):
                        return e.tensor_add(
                            xo[par][:, jc * GS:(jc + 1) * GS], pp_[pp2][:, :],
                            xr[par][:, jc * GS:(jc + 1) * GS])
                dve.op(fe, ins=(bP[pp2], bX[par], bC), outs=(bO[par],))
            sp.dma(lambda e, ti=ti, par=par: e.dma_start(
                dst[:, ti * GS:(ti + 1) * GS].rearrange("(c p) n -> p c n",
                                                        p=128),
                xo[par][:, :]), Qo[par], ins=(bO[par],))
        ctx.barrier()
        stk.close()
        pstk.close()

    def phase_ffn(tag, src, dst):
        ctx.new_phase_sems()
        stk, pstk = ExitStack(), ExitStack()
        NTI = NTOK // FT
        NGR = FT // 512 if FT >= 512 else 1
        GW = min(512, FT)
        NTG = FT // 128
        ln_2 = [sbuf(f"{tag}_ln{i}", [128, HC * FT], BF16, stk)
                for i in range(2)]
        bLn_2 = [ctx.buf(f"{tag}_ln{i}") for i in range(2)]
        routed = sbuf(f"{tag}_rt", [128, HC * FT], BF16, stk)
        bRt = ctx.buf(f"{tag}_rt")
        c_all = sbuf(f"{tag}_ca", [8, FT], BF16, stk)
        bCa = ctx.buf(f"{tag}_ca")
        glog = sbuf(f"{tag}_gl", [128, 2 * E + 2], F32, stk)
        bGl = ctx.buf(f"{tag}_gl")
        gmsk = sbuf(f"{tag}_gm", [128, E], BF16, stk)
        bGm = ctx.buf(f"{tag}_gm")
        xs = [sbuf(f"{tag}_xs{i}", [128, HC * FT], BF16, stk) for i in range(2)]
        bXs = [ctx.buf(f"{tag}_xs{i}") for i in range(2)]
        act_t = [sbuf(f"{tag}_at{i}", [128, IC2 * GW], BF16, stk)
                 for i in range(2)]
        bAt = [ctx.buf(f"{tag}_at{i}") for i in range(2)]
        sg_sb = [sbuf(f"{tag}_sg{i}", [128, GW], BF16, stk) for i in range(2)]
        bSg = [ctx.buf(f"{tag}_sg{i}") for i in range(2)]
        wg = [sbuf(f"{tag}_wg{i}", [128, HC * IH], BF16, stk) for i in range(2)]
        wu = [sbuf(f"{tag}_wu{i}", [128, HC * IH], BF16, stk) for i in range(2)]
        wd = [sbuf(f"{tag}_wd{i}", [128, IC2 * H], BF16, stk) for i in range(2)]
        bWg = [ctx.buf(f"{tag}_wg{i}") for i in range(2)]
        bWu = [ctx.buf(f"{tag}_wu{i}") for i in range(2)]
        bWd = [ctx.buf(f"{tag}_wd{i}") for i in range(2)]
        xo1 = sbuf(f"{tag}_xo", [128, HC * GW], F32, stk)
        xo = [xo1, xo1]
        bXo1 = ctx.buf(f"{tag}_xo")
        bXo = [bXo1, bXo1]

        ps_ct = psum(f"{tag}_pct", [8, 128], BF16, pstk)
        bPct = ctx.buf(f"{tag}_pct")
        ps_g2 = [psum(f"{tag}_pg{i}", [128, GW], F32, pstk) for i in range(2)]
        bPg2 = [ctx.buf(f"{tag}_pg{i}") for i in range(2)]
        ps_u = psum(f"{tag}_pu", [128, GW], F32, pstk)
        bPu = ctx.buf(f"{tag}_pu")
        ps_d2 = [psum(f"{tag}_pd{i}", [128, GW], F32, pstk) for i in range(2)]
        bPd2 = [ctx.buf(f"{tag}_pd{i}") for i in range(2)]
        kit = LNKit(f"{tag}_lnk", FT if FT <= 512 else 512, stk, pstk)
        ps_rep, bPrep = kit.pb, kit.b_pb
        Qw = [ctx.queue(tag + "_qw0"), ctx.queue(tag + "_qw1")]
        Qxr = [ctx.queue(tag + "_qxr0"), ctx.queue(tag + "_qxr1")]
        Qxo = [ctx.queue(tag + "_qxo0"), ctx.queue(tag + "_qxo1")]
        gw32 = sbuf(f"{tag}_gw32", [128, HC * E], F32, stk)
        bGw32 = ctx.buf(f"{tag}_gw32")
        gsum = sbuf(f"{tag}_gsum", [128, E + 1], F32, stk)
        bGsum = ctx.buf(f"{tag}_gsum")
        glc = sbuf(f"{tag}_glc", [128, 2], F32, stk)
        bGlc = ctx.buf(f"{tag}_glc")
        sp.dma(lambda e: e.dma_start(
            gw32[:, :], gate_w.rearrange("(c p) n -> p c n", p=128)),
            ctx.queue(tag + "_gw"), outs=(bGw32,))
        def fcs(e):
            for c in range(HC):
                i = e.matmul(kit.pa[0:1, 0:E], g_ocf[:, :],
                             gw32[:, c * E:(c + 1) * E],
                             start=(c == 0), stop=(c == HC - 1))
            return i
        pe.op(fcs, ins=(bGw32, bC), outs=(kit.b_pa,))
        dve.op(lambda e: e.tensor_copy(gsum[0:1, E:E + 1].rearrange(
            "a b -> a b") if False else gsum[0:1, 0:E], kit.pa[0:1, 0:E]),
            ins=(kit.b_pa,), outs=(bGsum,))
        pe.op(lambda e: e.matmul(kit.pa[:, 0:E], g_orf[:, :], gsum[0:1, 0:E],
                                 start=True, stop=True),
              ins=(bGsum, bC), outs=(kit.b_pa,))
        dve.op(lambda e: e.tensor_copy(gsum[:, 0:E], kit.pa[:, 0:E]),
               ins=(kit.b_pa,), outs=(bGsum,))

        e_order = [8] + list(range(8))

        for ti in range(NTI):
            col0 = ti * FT
            ln_t, bLn = ln_2[ti % 2], bLn_2[ti % 2]
            for cg in range(cdiv(FT, 512)):
                w = min(FT, (cg + 1) * 512) - cg * 512
                emit_ln_into(kit, src, col0 + cg * 512, ln_t, bLn, FT,
                             cg * 512, w)
                for tgl in range(w // 128):
                    tg = (cg * 512) // 128 + tgl
                    def fg(e, tgl=tgl, w=w):
                        e.matmul(kit.pa[:, 0:1],
                                 kit.rr[0:1, tgl * 128:(tgl + 1) * 128],
                                 g_orf[0:1, 0:1], start=True, stop=True)
                        e.matmul(kit.pa[:, 1:2],
                                 kit.rb[0:1, tgl * 128:(tgl + 1) * 128],
                                 g_orf[0:1, 0:1], start=True, stop=True)
                        for c in range(HC):
                            i = e.matmul(kit.pa[:, 2:2 + E],
                                         kit.xt[:, c * w + tgl * 128:
                                                c * w + tgl * 128 + 128],
                                         gw32[:, c * E:(c + 1) * E],
                                         start=(c == 0), stop=(c == HC - 1))
                        return i
                    pe.op(fg, ins=(kit.b_rr, kit.b_rb, kit.b_xt, bGw32, bC),
                          outs=(kit.b_pa,))
                    dve.op(lambda e: e.tensor_copy(glc[:, :], kit.pa[:, 0:2]),
                           ins=(kit.b_pa,), outs=(bGlc,))
                    dve.op(lambda e: e.tensor_scalar(
                        glog[:, 0:E], kit.pa[:, 2:2 + E], glc[:, 0:1], None,
                        op0=ALU.mult),
                        ins=(kit.b_pa, bGlc), outs=(bGl,))
                    dve.op(lambda e: e.tensor_scalar(
                        glog[:, E + 2:2 * E + 2], gsum[:, 0:E], glc[:, 1:2],
                        None, op0=ALU.mult),
                        ins=(bGsum, bGlc), outs=(bGl,))
                    dve.op(lambda e: e.tensor_sub(
                        glog[:, 0:E], glog[:, 0:E], glog[:, E + 2:2 * E + 2]),
                        ins=(bGl,), outs=(bGl,))
                    dve.op(lambda e: e.reduce_max(glog[:, E:E + 1],
                                                  glog[:, 0:E], axis=AX.X),
                           ins=(bGl,), outs=(bGl,))
                    act.op(lambda e: e.activation(glog[:, E + 1:E + 2],
                                                  glog[:, E:E + 1], AF.Sigmoid),
                           ins=(bGl,), outs=(bGl,))
                    dve.op(lambda e: e.tensor_scalar(
                        gmsk[:, :], glog[:, 0:E], glog[:, E:E + 1], None,
                        op0=ALU.is_equal), ins=(bGl,), outs=(bGm,))
                    dve.op(lambda e: e.tensor_scalar(
                        gmsk[:, :], gmsk[:, :], glog[:, E + 1:E + 2], None,
                        op0=ALU.mult), ins=(bGm, bGl), outs=(bGm,))
                    pe.op(lambda e: e.transpose(ps_ct[:, :], gmsk[:, :],
                                                g_id[:, :]),
                          ins=(bGm, bC), outs=(bPct,))
                    dve.op(lambda e, tg=tg: e.tensor_copy(
                        c_all[:, tg * 128:(tg + 1) * 128], ps_ct[:, :]),
                        ins=(bPct,), outs=(bCa,))
            def emit_xs(e_idx, epar, ln_t=ln_t, bLn=bLn):
                for grp in range(NGR):
                    gn0 = grp * GW
                    pe.op(lambda e, e_idx=e_idx, gn0=gn0: e.matmul(
                        ps_rep[:, 0:GW],
                        g_esel[:, e_idx * 128:(e_idx + 1) * 128],
                        c_all[:, gn0:gn0 + GW], start=True, stop=True),
                        ins=(bCa, bC), outs=(bPrep,))
                    def fxs(e, gn0=gn0, epar=epar, ln_t=ln_t):
                        for c in range(HC):
                            i = e.tensor_mul(
                                xs[epar][:, c * FT + gn0:c * FT + gn0 + GW],
                                ln_t[:, c * FT + gn0:c * FT + gn0 + GW],
                                ps_rep[:, 0:GW])
                        return i
                    dve.op(fxs, ins=(bLn, bPrep), outs=(bXs[epar],))
            for ei, e_idx in enumerate(e_order):
                if ei + 1 < len(e_order):
                    emit_xs(e_order[ei + 1], (ei + 1) % 2)
                epar = ei % 2
                for half in range(SPL):
                    wpar = (ei * SPL + half) % 2
                    if e_idx == 8:
                        srcs = (sh_gate[:, half * IH:(half + 1) * IH],
                                sh_up[:, half * IH:(half + 1) * IH],
                                sh_down[half * IH:(half + 1) * IH, :])
                    else:
                        srcs = (rt_gate[e_idx, :, half * IH:(half + 1) * IH],
                                rt_up[e_idx, :, half * IH:(half + 1) * IH],
                                rt_down[e_idx, half * IH:(half + 1) * IH, :])
                    pool.dma(lambda e, s=srcs[0], wpar=wpar: e.dma_start(
                        wg[wpar][:, :], s.rearrange("(c p) n -> p c n", p=128)),
                        Qw[wpar], outs=(bWg[wpar],))
                    pool.dma(lambda e, s=srcs[1], wpar=wpar: e.dma_start(
                        wu[wpar][:, :], s.rearrange("(c p) n -> p c n", p=128)),
                        Qw[wpar], outs=(bWu[wpar],))
                    pool.dma(lambda e, s=srcs[2], wpar=wpar: e.dma_start(
                        wd[wpar][:, :], s.rearrange("(c p) n -> p c n", p=128)),
                        Qw[wpar], outs=(bWd[wpar],))
                    m = (Qw[wpar], Qw[wpar].count)
                    bWg[wpar].w = bWu[wpar].w = bWd[wpar].w = m
                    for grp in range(NGR):
                        gn0 = grp * GW
                        use_ln = (e_idx == 8)
                        apar = (ei * SPL + half) % 2
                        for ic in range(IC2):
                            sgp = ic % 2
                            gp = ic % 2
                            ps_g, bPg = ps_g2[gp], bPg2[gp]
                            def fgm(e, ic=ic, gn0=gn0, wpar=wpar, epar=epar,
                                    use_ln=use_ln, ps_g=ps_g, ln_t=ln_t):
                                for c in range(HC):
                                    rhs = (ln_t[:, c * FT + gn0:c * FT + gn0 + GW]
                                           if use_ln else
                                           xs[epar][:, c * FT + gn0:
                                                    c * FT + gn0 + GW])
                                    i = e.matmul(
                                        ps_g[:, :],
                                        wg[wpar][:, c * IH + ic * 128:
                                                 c * IH + (ic + 1) * 128],
                                        rhs, start=(c == 0), stop=(c == HC - 1))
                                return i
                            def fum(e, ic=ic, gn0=gn0, wpar=wpar, epar=epar,
                                    use_ln=use_ln, ln_t=ln_t):
                                for c in range(HC):
                                    rhs = (ln_t[:, c * FT + gn0:c * FT + gn0 + GW]
                                           if use_ln else
                                           xs[epar][:, c * FT + gn0:
                                                    c * FT + gn0 + GW])
                                    i = e.matmul(
                                        ps_u[:, :],
                                        wu[wpar][:, c * IH + ic * 128:
                                                 c * IH + (ic + 1) * 128],
                                        rhs, start=(c == 0), stop=(c == HC - 1))
                                return i
                            ins_x = (bWg[wpar], bLn) if use_ln else (bWg[wpar], bXs[epar])
                            ins_xu = (bWu[wpar], bLn) if use_ln else (bWu[wpar], bXs[epar])
                            pe.op(fgm, ins=ins_x, outs=(bPg,))
                            pe.op(fum, ins=ins_xu, outs=(bPu,))
                            act.op(lambda e, sgp=sgp, ps_g=ps_g: e.activation(
                                sg_sb[sgp][:, :], ps_g[:, :], AF.Silu),
                                ins=(bPg,), outs=(bSg[sgp],))
                            dve.op(lambda e, ic=ic, sgp=sgp, apar=apar:
                                   e.tensor_mul(
                                       act_t[apar][:, ic * GW:(ic + 1) * GW],
                                       sg_sb[sgp][:, :], ps_u[:, :]),
                                   ins=(bSg[sgp], bPu), outs=(bAt[apar],))
                        first = (ei == 0 and half == 0)
                        for hc in range(HC):
                            dp = hc % 2
                            ps_d, bPd = ps_d2[dp], bPd2[dp]
                            def fdm(e, hc=hc, wpar=wpar, apar=apar, ps_d=ps_d):
                                for ic in range(IC2):
                                    i = e.matmul(
                                        ps_d[:, :],
                                        wd[wpar][:, ic * H + hc * 128:
                                                 ic * H + (hc + 1) * 128],
                                        act_t[apar][:, ic * GW:(ic + 1) * GW],
                                        start=(ic == 0), stop=(ic == IC2 - 1))
                                return i
                            pe.op(fdm, ins=(bWd[wpar], bAt[apar]), outs=(bPd,))
                            if first:
                                dve.op(lambda e, hc=hc, gn0=gn0, ps_d=ps_d:
                                       e.tensor_copy(
                                           routed[:, hc * FT + gn0:
                                                  hc * FT + gn0 + GW],
                                           ps_d[:, :]),
                                       ins=(bPd,), outs=(bRt,))
                            else:
                                def fac(e, hc=hc, gn0=gn0, ps_d=ps_d):
                                    return e.tensor_add(
                                        routed[:, hc * FT + gn0:hc * FT + gn0 + GW],
                                        ps_d[:, :],
                                        routed[:, hc * FT + gn0:hc * FT + gn0 + GW])
                                dve.op(fac, ins=(bPd, bRt), outs=(bRt,))
            for grp in range(NGR):
                gn0 = grp * GW
                par = grp % 2
                sp.dma(lambda e, gn0=gn0, col0=col0, par=par: e.dma_start(
                    xo[par][:, 0:HC * GW],
                    src[:, col0 + gn0:col0 + gn0 + GW].rearrange(
                        "(c p) n -> p c n", p=128)),
                    Qxr[0], outs=(bXo[par],))
                for hc in range(HC):
                    def fr(e, hc=hc, gn0=gn0, par=par):
                        return e.tensor_add(
                            xo[par][:, hc * GW:(hc + 1) * GW],
                            xo[par][:, hc * GW:(hc + 1) * GW],
                            routed[:, hc * FT + gn0:hc * FT + gn0 + GW])
                    dve.op(fr, ins=(bXo[par], bRt), outs=(bXo[par],))
                sp.dma(lambda e, gn0=gn0, par=par, col0=col0: e.dma_start(
                    dst[:, col0 + gn0:col0 + gn0 + GW].rearrange(
                        "(c p) n -> p c n", p=128),
                    xo[par][:, :]), Qxo[par], ins=(bXo[par],))
        ctx.barrier()
        stk.close()
        pstk.close()

    def emit_ln_into(kit, src_dram, col0, ln_sb, b_ln, FTT, off, w):
        xt, xb, sq = kit.xt, kit.xb, kit.sq
        TT = kit.TT
        assert w <= TT
        sp.dma(lambda e: e.dma_start(
            xt[:, 0:HC * w],
            src_dram[:, col0:col0 + w].rearrange("(c p) n -> p c n", p=128)),
            kit.qx, outs=(kit.b_xt,))
        act.op(lambda e: e.copy(xb[:, 0:HC * w], xt[:, 0:HC * w]),
               ins=(kit.b_xt,), outs=(kit.b_xb,))
        act.op(lambda e: e.activation(sq[:, 0:HC * w], xb[:, 0:HC * w],
                                      AF.Square),
               ins=(kit.b_xb,), outs=(kit.b_sq,))
        rH = 1.0 / H
        def fm(e):
            for c in range(HC):
                i = e.matmul(kit.pm[0:1, 0:w], g_oc[:, :],
                             xb[:, c * w:(c + 1) * w],
                             start=(c == 0), stop=(c == HC - 1))
            return i
        def fs(e):
            for c in range(HC):
                i = e.matmul(kit.ps[0:1, 0:w], g_oc[:, :],
                             sq[:, c * w:(c + 1) * w],
                             start=(c == 0), stop=(c == HC - 1))
            return i
        pe.op(fm, ins=(kit.b_xb, bC), outs=(kit.b_pm,))
        pe.op(fs, ins=(kit.b_sq, bC), outs=(kit.b_ps,))
        dve.op(lambda e: e.tensor_scalar(kit.rm[0:1, 0:w], kit.pm[0:1, 0:w],
                                         rH, None, op0=ALU.mult),
               ins=(kit.b_pm,), outs=(kit.b_rm,))
        dve.op(lambda e: e.tensor_scalar(kit.rv[0:1, 0:w], kit.ps[0:1, 0:w],
                                         rH, None, op0=ALU.mult),
               ins=(kit.b_ps,), outs=(kit.b_rv,))
        dve.op(lambda e: e.tensor_mul(kit.rb[0:1, 0:w], kit.rm[0:1, 0:w],
                                      kit.rm[0:1, 0:w]),
               ins=(kit.b_rm,), outs=(kit.b_rb,))
        dve.op(lambda e: e.tensor_sub(kit.rv[0:1, 0:w], kit.rv[0:1, 0:w],
                                      kit.rb[0:1, 0:w]),
               ins=(kit.b_rv, kit.b_rb), outs=(kit.b_rv,))
        dve.op(lambda e: e.tensor_scalar(kit.rv[0:1, 0:w], kit.rv[0:1, 0:w],
                                         1e-5, None, op0=ALU.add),
               ins=(kit.b_rv,), outs=(kit.b_rv,))
        act.op(lambda e: e.activation(kit.rr[0:1, 0:w], kit.rv[0:1, 0:w],
                                      AF.Ln),
               ins=(kit.b_rv,), outs=(kit.b_rr,))
        act.op(lambda e: e.activation(kit.rr[0:1, 0:w], kit.rr[0:1, 0:w],
                                      AF.Exp, scale=-0.5),
               ins=(kit.b_rr,), outs=(kit.b_rr,))
        dve.op(lambda e: e.tensor_mul(kit.rb[0:1, 0:w], kit.rm[0:1, 0:w],
                                      kit.rr[0:1, 0:w]),
               ins=(kit.b_rm, kit.b_rr), outs=(kit.b_rb,))
        pe.op(lambda e: e.matmul(kit.pa[:, 0:w], g_orf[:, :], kit.rr[0:1, 0:w],
                                 start=True, stop=True),
              ins=(kit.b_rr, bC), outs=(kit.b_pa,))
        pe.op(lambda e: e.matmul(kit.pb[:, 0:w], g_orf[:, :], kit.rb[0:1, 0:w],
                                 start=True, stop=True),
              ins=(kit.b_rb, bC), outs=(kit.b_pb,))
        for c in range(HC):
            dve.op(lambda e, c=c: e.tensor_mul(
                sq[:, c * w:(c + 1) * w], xt[:, c * w:(c + 1) * w],
                kit.pa[:, 0:w]),
                ins=(kit.b_xt, kit.b_pa), outs=(kit.b_sq,))
            dve.op(lambda e, c=c: e.tensor_sub(
                ln_sb[:, c * FTT + off:c * FTT + off + w],
                sq[:, c * w:(c + 1) * w], kit.pb[:, 0:w]),
                ins=(kit.b_sq, kit.b_pb), outs=(b_ln,))

    def phase_p5():
        ctx.new_phase_sems()
        stk, pstk = ExitStack(), ExitStack()
        wq_sb = sbuf("p5_wq", [128, HC * H], BF16, stk)
        wk_sb = sbuf("p5_wk", [128, HC * H], BF16, stk)
        wv_sb = sbuf("p5_wv", [128, HC * H], BF16, stk)
        bWq, bWk, bWv = (ctx.buf("p5_wq"), ctx.buf("p5_wk"), ctx.buf("p5_wv"))
        Qpw = ctx.queue("p5_w")
        pool.dma(lambda e: e.dma_start(
            wq_sb[:, :], wq.rearrange("(c p) n -> p c n", p=128)), Qpw,
            outs=(bWq,))
        pool.dma(lambda e: e.dma_start(
            wk_sb[:, :], wk.rearrange("(c p) n -> p c n", p=128)), Qpw,
            outs=(bWk,))
        pool.dma(lambda e: e.dma_start(
            wv_sb[:, :], wv.rearrange("(c p) n -> p c n", p=128)), Qpw,
            outs=(bWv,))
        m = (Qpw, Qpw.count)
        bWq.w = bWk.w = bWv.w = m
        Qs5 = [ctx.queue("p5_s0"), ctx.queue("p5_s1")]
        kit = LNKit("p5ln", GS, stk, pstk)
        ln = [sbuf(f"p5_ln{i}", [128, HC * GS], BF16, stk) for i in range(2)]
        bL = [ctx.buf(f"p5_ln{i}") for i in range(2)]
        pq = [psum(f"p5_pq{i}", [128, GS], F32, pstk) for i in range(2)]
        bQ = [ctx.buf(f"p5_pq{i}") for i in range(2)]
        qs = [sbuf(f"p5_qs{i}", [128, GS], BF16, stk) for i in range(2)]
        bqs = [ctx.buf(f"p5_qs{i}") for i in range(2)]
        rot = sbuf("p5_rot", [128, GS], BF16, stk)
        brot = ctx.buf("p5_rot")
        rp2 = sbuf("p5_rp2", [128, GS], BF16, stk)
        brp2 = ctx.buf("p5_rp2")

        def rope(qsb, bqsb, pos0, w0):
            cw = GS - w0
            cc0 = pos0 + w0 - T // 2
            def fr1(e):
                e.tensor_scalar(rot[0:32, w0:GS], qsb[32:64, w0:GS], -1.0,
                                None, op0=ALU.mult)
                e.tensor_copy(rot[32:64, w0:GS], qsb[0:32, w0:GS])
                e.tensor_scalar(rot[64:96, w0:GS], qsb[96:128, w0:GS], -1.0,
                                None, op0=ALU.mult)
                return e.tensor_copy(rot[96:128, w0:GS], qsb[64:96, w0:GS])
            dve.op(fr1, ins=(bqsb,), outs=(brot,))
            dve.op(lambda e: e.tensor_mul(rp2[:, w0:GS], rot[:, w0:GS],
                                          g_sin[:, cc0:cc0 + cw]),
                   ins=(brot, bC), outs=(brp2,))
            dve.op(lambda e: e.tensor_mul(rot[:, w0:GS], qsb[:, w0:GS],
                                          g_cos[:, cc0:cc0 + cw]),
                   ins=(bqsb, bC), outs=(brot,))
            dve.op(lambda e: e.tensor_add(qsb[:, w0:GS], rot[:, w0:GS],
                                          rp2[:, w0:GS]),
                   ins=(brot, brp2), outs=(bqsb,))

        for ti in range(NTOK // GS):
            par = ti % 2
            emit_ln(kit, X, ti * GS, ln[par], bL[par])
            pos0 = (ti * GS) % T
            ropew = None
            if pos0 + GS > T // 2:
                ropew = max(0, T // 2 - pos0)
            for which, wsb, bW, dstD, has_b in (
                    ("q", wq_sb, bWq, Qb, True), ("k", wk_sb, bWk, Kb, False)):
                for jc in range(HC):
                    pp = jc % 2
                    def fm(e, jc=jc, par=par, pp=pp, wsb=wsb):
                        for c in range(HC):
                            i = e.matmul(pq[pp][:, :],
                                         wsb[:, c * H + jc * 128:
                                             c * H + (jc + 1) * 128],
                                         ln[par][:, c * GS:(c + 1) * GS],
                                         start=(c == 0), stop=(c == HC - 1))
                        return i
                    pe.op(fm, ins=(bW, bL[par]), outs=(bQ[pp],))
                    if has_b:
                        def fe(e, jc=jc, pp=pp):
                            return e.tensor_scalar(
                                qs[pp][:, :], pq[pp][:, :],
                                g_bq[:, jc:jc + 1], scal,
                                op0=ALU.add, op1=ALU.mult)
                        dve.op(fe, ins=(bQ[pp], bC), outs=(bqs[pp],))
                    else:
                        act.op(lambda e, pp=pp: e.copy(qs[pp][:, :],
                                                       pq[pp][:, :]),
                               ins=(bQ[pp],), outs=(bqs[pp],))
                    if ropew is not None:
                        rope(qs[pp], bqs[pp], pos0, ropew)
                    sp.dma(lambda e, jc=jc, ti=ti, pp=pp, dstD=dstD:
                           e.dma_start(
                               dstD[jc * 128:(jc + 1) * 128,
                                    ti * GS:(ti + 1) * GS], qs[pp][:, :]),
                           Qs5[pp], ins=(bqs[pp],))
            for tg in range(GS // 128):
                g_glob = (ti * GS) // 128 + tg
                for nh2 in range(cdiv(H, 512)):
                    jn0 = nh2 * 512
                    jw = min(H, jn0 + 512) - jn0
                    pp = tg % 2
                    def fv(e, tg=tg, jn0=jn0, jw=jw, par=par, pp=pp):
                        for c in range(HC):
                            e.matmul(pq[pp][:, 0:jw],
                                     ln[par][:, c * GS + tg * 128:
                                             c * GS + (tg + 1) * 128],
                                     wv_sb[:, c * H + jn0:c * H + jn0 + jw],
                                     start=(c == 0), stop=False)
                        return e.matmul(pq[pp][:, 0:jw], g_or[0:1, :],
                                        g_bv[0:1, jn0:jn0 + jw],
                                        start=False, stop=True)
                    pe.op(fv, ins=(bWv, bL[par], bC), outs=(bQ[pp],))
                    act.op(lambda e, pp=pp, jw=jw: e.copy(
                        qs[pp][:, 0:jw], pq[pp][:, 0:jw]),
                        ins=(bQ[pp],), outs=(bqs[pp],))
                    for hh in range(jw // HS):
                        hA = jn0 // HS + hh
                        sp.dma(lambda e, hA=hA, g_glob=g_glob, pp=pp, hh=hh:
                               e.dma_start(Vb[hA, g_glob],
                                           qs[pp][:, hh * HS:(hh + 1) * HS]),
                               Qs5[pp], ins=(bqs[pp],))
        ctx.barrier()
        stk.close()
        pstk.close()

    def drive(gens, W=2):
        from collections import deque
        act_ = deque()
        it = iter(gens)
        while True:
            while len(act_) < W:
                g = next(it, None)
                if g is None:
                    break
                act_.append(g)
            if not act_:
                break
            g = act_.popleft()
            try:
                next(g)
                act_.append(g)
            except StopIteration:
                pass

    def phase_p6():
        ctx.new_phase_sems()
        stk, pstk = ExitStack(), ExitStack()
        NKB = T // 128
        NQB = cdiv(T, 512)
        QB = min(512, T)
        HS1 = HS + 1
        NPAR = 4
        def mk2(nm, shape, dtyp):
            return ([sbuf(f"{nm}{i}", shape, dtyp, stk) for i in range(NPAR)],
                    [ctx.buf(f"{nm}{i}") for i in range(NPAR)])
        Qlq = [ctx.queue(f"p6_lq{i}") for i in range(NPAR)]
        Qlk = [ctx.queue(f"p6_lk{i}") for i in range(NPAR)]
        Qlv = [ctx.queue(f"p6_lv{i}") for i in range(NPAR)]
        Qao = [ctx.queue(f"p6_ao{i}") for i in range(NPAR)]
        q2, bq2 = mk2("p6_q2", [66, T], BF16)
        k2, bk2 = mk2("p6_k2", [66, T], BF16)
        vtm, bvtm = mk2("p6_vtm", [128, NKB * HS1], BF16)
        sqs = [sbuf(f"p6_sqs{i}", [64, T], BF16, stk) for i in range(3)]
        bsqs = [ctx.buf(f"p6_sqs{i}") for i in range(3)]
        qn, bqn = mk2("p6_qn", [1, T], F32)
        nC, bnC = mk2("p6_nC", [1, T], F32)
        nCh, bnCh = mk2("p6_nCh", [1, T], BF16)
        mk2v, bmk2 = mk2("p6_mk2", [1, 2], F32)
        attb = [sbuf(f"p6_attb{i}", [128, NKB * QB], BF16, stk)
                for i in range(3)]
        battb = [ctx.buf(f"p6_attb{i}") for i in range(3)]
        osb, bosb = mk2("p6_osb", [64, QB], BF16)
        den, bden = mk2("p6_den", [1, QB], BF16)
        recb = [sbuf(f"p6_recb{i}", [64, QB], F32, stk) for i in range(3)]
        brecb = [ctx.buf(f"p6_recb{i}") for i in range(3)]
        ps2 = [psum(f"p6_ps2{i}", [128, QB], F32, pstk) for i in range(4)]
        bps2 = [ctx.buf(f"p6_ps2{i}") for i in range(4)]
        pso2 = [psum(f"p6_pso{i}", [65, QB], F32, pstk) for i in range(2)]
        bpso2 = [ctx.buf(f"p6_pso{i}") for i in range(2)]
        psrm = [psum(f"p6_psrm{i}", [65, 512], F32, pstk) for i in range(2)]
        bpsrm = [ctx.buf(f"p6_psrm{i}") for i in range(2)]
        for p in range(NPAR):
            dve.op(lambda e, p=p: e.memset(
                vtm[p][:, HS:NKB * HS1:HS1], 1.0), outs=(bvtm[p],))
            dve.op(lambda e, p=p: e.memset(k2[p][64:66, :], 1.0),
                   outs=(bk2[p],))
            dve.op(lambda e, p=p: e.memset(q2[p][64:66, :], 0.0),
                   outs=(bq2[p],))

        def gen(it_idx, s, h):
            p = it_idx % NPAR
            a3 = it_idx % 3
            slot = it_idx % 2
            c0 = s * T
            sp.dma(lambda e, h=h, c0=c0, p=p: e.dma_start(
                q2[p][0:64, :], Qb[h * HS:(h + 1) * HS, c0:c0 + T]),
                Qlq[p], outs=(bq2[p],))
            sp.dma(lambda e, h=h, c0=c0, p=p: e.dma_start(
                k2[p][0:64, :], Kb[h * HS:(h + 1) * HS, c0:c0 + T]),
                Qlk[p], outs=(bk2[p],))
            g0 = c0 // 128
            sp.dma(lambda e, h=h, g0=g0, p=p: e.dma_start(
                vtm[p][:, :].rearrange("q (g c) -> q g c", c=HS1)[:, :, 0:HS],
                Vb[h, g0:g0 + NKB].rearrange("g q c -> q g c")),
                Qlv[p], outs=(bvtm[p],))
            yield
            act.op(lambda e, p=p, a3=a3: e.activation(
                sqs[a3][:, :], q2[p][0:64, :], AF.Square),
                   ins=(bq2[p],), outs=(bsqs[a3],))
            for cg in range(T // 512):
                n0 = cg * 512
                pe.op(lambda e, n0=n0, a3=a3, slot=slot: e.matmul(
                    psrm[slot][64:65, :], g_oc[0:64, :], sqs[a3][:, n0:n0 + 512],
                    start=True, stop=True),
                    ins=(bsqs[a3], bC), outs=(bpsrm[slot],))
                dve.op(lambda e, n0=n0, p=p, slot=slot: e.tensor_copy(
                    qn[p][0:1, n0:n0 + 512], psrm[slot][64:65, :]),
                    ins=(bpsrm[slot],), outs=(bqn[p],))
            yield
            act.op(lambda e, p=p, a3=a3: e.activation(
                sqs[a3][:, :], k2[p][0:64, :], AF.Square),
                   ins=(bk2[p],), outs=(bsqs[a3],))
            for cg in range(T // 512):
                n0 = cg * 512
                pe.op(lambda e, n0=n0, a3=a3, slot=slot: e.matmul(
                    psrm[slot][64:65, :], g_oc[0:64, :], sqs[a3][:, n0:n0 + 512],
                    start=True, stop=True),
                    ins=(bsqs[a3], bC), outs=(bpsrm[slot],))
                dve.op(lambda e, n0=n0, p=p, slot=slot: e.tensor_copy(
                    nC[p][0:1, n0:n0 + 512], psrm[slot][64:65, :]),
                    ins=(bpsrm[slot],), outs=(bnC[p],))
            yield
            dve.op(lambda e, p=p: e.reduce_max(mk2v[p][:, 0:1],
                                               nC[p][0:1, :], axis=AX.X),
                   ins=(bnC[p],), outs=(bmk2[p],))
            act.op(lambda e, p=p: e.activation(mk2v[p][:, 1:2],
                                               mk2v[p][:, 0:1], AF.Ln),
                   ins=(bmk2[p],), outs=(bmk2[p],))
            act.op(lambda e, p=p: e.activation(nC[p][0:1, :], qn[p][0:1, :],
                                               AF.Ln),
                   ins=(bqn[p],), outs=(bnC[p],))
            dve.op(lambda e, p=p: e.tensor_scalar(
                nC[p][0:1, :], nC[p][0:1, :], mk2v[p][0:1, 1:2], None,
                op0=ALU.add),
                ins=(bnC[p], bmk2[p]), outs=(bnC[p],))
            act.op(lambda e, p=p: e.activation(nC[p][0:1, :], nC[p][0:1, :],
                                               AF.Exp, scale=0.5),
                   ins=(bnC[p],), outs=(bnC[p],))
            dve.op(lambda e, p=p: e.tensor_scalar(
                nC[p][0:1, :], nC[p][0:1, :], -1.0, None, op0=ALU.mult),
                ins=(bnC[p],), outs=(bnC[p],))
            act.op(lambda e, p=p: e.copy(nCh[p][:, :], nC[p][0:1, :]),
                   ins=(bnC[p],), outs=(bnCh[p],))
            for cg in range(T // 512):
                n0 = cg * 512
                pb = slot * 2
                pe.op(lambda e, n0=n0, p=p, pb=pb: e.matmul(
                    ps2[pb][:, :], g_e64[0:1, :], nCh[p][0:1, n0:n0 + 512],
                    start=True, stop=True),
                    ins=(bnCh[p], bC), outs=(bps2[pb],))
                dve.op(lambda e, n0=n0, p=p, pb=pb: e.tensor_copy(
                    q2[p][64:65, n0:n0 + 512], ps2[pb][64:65, :]),
                    ins=(bps2[pb],), outs=(bq2[p],))
            yield
            for qb in range(NQB):
                qn0 = qb * QB
                pso, bpso = pso2[slot], bpso2[slot]
                for kb in range(NKB):
                    pb2 = slot * 2 + kb % 2
                    def f5(e, kb=kb, qn0=qn0, p=p, pb2=pb2):
                        return e.matmul(
                            ps2[pb2][:, :],
                            k2[p][:, kb * 128:(kb + 1) * 128],
                            q2[p][:, qn0:qn0 + QB], start=True, stop=True)
                    pe.op(f5, ins=(bk2[p], bq2[p]), outs=(bps2[pb2],))
                    act.op(lambda e, kb=kb, a3=a3, pb2=pb2: e.activation(
                        attb[a3][:, kb * QB:(kb + 1) * QB], ps2[pb2][:, :],
                        AF.Exp),
                        ins=(bps2[pb2],), outs=(battb[a3],))
                    yield
                def fav(e, p=p, a3=a3, pso=pso):
                    for kb in range(NKB):
                        i = e.matmul(
                            pso[:, :], vtm[p][:, kb * HS1:(kb + 1) * HS1],
                            attb[a3][:, kb * QB:(kb + 1) * QB],
                            start=(kb == 0), stop=(kb == NKB - 1))
                    return i
                pe.op(fav, ins=(bvtm[p], battb[a3]), outs=(bpso,))
                dve.op(lambda e, p=p, pso=pso: e.tensor_copy(
                    den[p][0:1, :], pso[64:65, :]),
                    ins=(bpso,), outs=(bden[p],))
                pe.op(lambda e, p=p, slot=slot: e.matmul(
                    psrm[slot][0:64, 0:QB], g_or[0:1, 0:64], den[p][0:1, :],
                    start=True, stop=True),
                    ins=(bden[p], bC), outs=(bpsrm[slot],))
                dve.op(lambda e, a3=a3, slot=slot: e.reciprocal(
                    recb[a3][:, :], psrm[slot][0:64, 0:QB]),
                    ins=(bpsrm[slot],), outs=(brecb[a3],))
                dve.op(lambda e, p=p, a3=a3, pso=pso: e.tensor_mul(
                    osb[p][:, :], pso[0:64, :], recb[a3][:, :]),
                    ins=(bpso, brecb[a3]), outs=(bosb[p],))
                sp.dma(lambda e, h=h, c0=c0, qn0=qn0, p=p: e.dma_start(
                    AO[h * HS:(h + 1) * HS, c0 + qn0:c0 + qn0 + QB],
                    osb[p][:, :]), Qao[p], ins=(bosb[p],))
                yield

        gens = []
        it_idx = 0
        for s in range(SEQ):
            for h in range(NH):
                gens.append(gen(it_idx, s, h))
                it_idx += 1
        drive(gens, W=3)
        ctx.barrier()
        stk.close()
        pstk.close()

    phase_fns = {
        "p1": phase_p1,
        "p2": phase_p2,
        "p3": lambda: phase_proj("p3", w_projT, x_in, X),
        "p4": lambda: phase_ffn("p4", X, X),
        "p5": phase_p5,
        "p6": phase_p6,
        "p7": lambda: phase_proj("p7", wo, X, X, bias_sb=g_bo),
        "p8": lambda: phase_ffn("p8", X, out_fm),
    }
    for ph in phases:
        phase_fns[ph]()

    ctx.barrier()

    with nc.Block() as block:
        @block.tensor
        def _(t):
            for f in ctx.pe.items:
                f(t)
        @block.vector
        def _(v):
            for f in ctx.dve.items:
                f(v)
        @block.scalar
        def _(s):
            for f in ctx.act.items:
                f(s)
        @block.gpsimd
        def _(g):
            for f in ctx.pool.items:
                f(g)
        @block.sync
        def _(sy):
            for f in ctx.sp.items:
                f(sy)

    esg.close()
    return nc


def make_in_maps(cfg, inputs, ncores=8):
    x = np.asarray(inputs["L0_attn_a_x"])
    B = x.shape[0]
    spc = B // ncores
    hostc = build_host_constants(cfg)
    w_attnT = np.ascontiguousarray(np.asarray(inputs["L0_attn_a_W_attn"]).T)
    w_projT = np.ascontiguousarray(np.asarray(inputs["L0_attn_a_W_proj"]).T)
    cosT = np.ascontiguousarray(np.asarray(inputs["L0_attn_b_cos"]).T)
    sinT = np.ascontiguousarray(np.asarray(inputs["L0_attn_b_sin"]).T)
    cos2 = np.ascontiguousarray(np.tile(cosT, (2, 1)))
    sin2 = np.ascontiguousarray(np.tile(sinT, (2, 1)))
    common = dict(
        w_attnT=w_attnT, w_projT=w_projT, cos2=cos2, sin2=sin2,
        wq=np.ascontiguousarray(inputs["L0_attn_b_wq"]),
        wk=np.ascontiguousarray(inputs["L0_attn_b_wk"]),
        wv=np.ascontiguousarray(inputs["L0_attn_b_wv"]),
        wo=np.ascontiguousarray(inputs["L0_attn_b_wo"]),
        bq=np.ascontiguousarray(
            np.asarray(inputs["L0_attn_b_bq"]).reshape(-1, 128).T),
        bv=np.ascontiguousarray(np.asarray(inputs["L0_attn_b_bv"])[:, None]),
        bo=np.ascontiguousarray(
            np.asarray(inputs["L0_attn_b_bo"]).reshape(-1, 128).T),
        gate_w=np.ascontiguousarray(inputs["L0_ffn_gate_w"]),
        sh_gate=np.ascontiguousarray(inputs["L0_ffn_sh_gate"]),
        sh_up=np.ascontiguousarray(inputs["L0_ffn_sh_up"]),
        sh_down=np.ascontiguousarray(inputs["L0_ffn_sh_down"]),
        rt_gate=np.ascontiguousarray(inputs["L0_ffn_rt_gate"]),
        rt_up=np.ascontiguousarray(inputs["L0_ffn_rt_up"]),
        rt_down=np.ascontiguousarray(inputs["L0_ffn_rt_down"]),
        **{k: np.ascontiguousarray(v) for k, v in hostc.items()},
    )
    in_maps = []
    for c in range(ncores):
        xc = x[c * spc:(c + 1) * spc]
        x_fm = np.ascontiguousarray(
            xc.transpose(2, 0, 1).reshape(cfg.H, spc * cfg.T))
        in_maps.append(dict(x_fm=x_fm.astype(np.float32), **common))
    return in_maps


def kernel(**inputs):
    cfg = REAL
    ncores = 8
    in_maps = make_in_maps(cfg, inputs, ncores)
    nc = build_program(cfg)
    res = run_bass_kernel_spmd(nc, in_maps, core_ids=list(range(ncores)))
    B = 32
    spc = B // ncores
    outs = []
    for c in range(ncores):
        o = res.results[c]["out_fm"]
        outs.append(o.reshape(cfg.H, spc, cfg.T).transpose(1, 2, 0))
    return np.ascontiguousarray(np.concatenate(outs, axis=0)).astype(np.float32)
